# version 01
# baseline (speedup 1.0000x reference)
import os
import numpy as np
from contextlib import ExitStack

import concourse.bass as bass
import concourse.mybir as mybir
from concourse.bass_utils import run_bass_kernel_spmd

F32 = mybir.dt.float32
F32R = mybir.dt.float32r
BF16 = mybir.dt.bfloat16
U8 = mybir.dt.uint8
AF = mybir.ActivationFunctionType
ALU = mybir.AluOpType
AX = mybir.AxisListType

NCORE = 8
D = 1024
H3 = 3072
NH = 16
HD = 64
TC = 2048
NCH = 4
TCH = 512
NTT = 4
NT = NCH * NTT
IFF = 4096
NE = 4
NKT = 8
NIC = 32
SEQ = 8192

KDEV = int(os.environ.get("KDEV_PHASES", "0"))
KDEV_NOROPE = int(os.environ.get("KDEV_NOROPE", "0"))
KDEV_B2B = int(os.environ.get("KDEV_B2B", "1"))


def build():
    nc = bass.Bass("TRN2", target_bir_lowering=False,
                   detect_race_conditions=(os.environ.get("KDEV_NORD") != "1"))

    x_d = nc.dram_tensor("x", [TC, D], F32, kind="ExternalInput")
    wqkv_d = nc.dram_tensor("wqkv", [D, H3], F32, kind="ExternalInput")
    woa_d = nc.dram_tensor("woa", [D, D], F32, kind="ExternalInput")
    boa_d = nc.dram_tensor("boa", [1, D], F32, kind="ExternalInput")
    wr_d = nc.dram_tensor("wr", [D, NE], F32, kind="ExternalInput")
    w1_d = nc.dram_tensor("w1", [NE, D, IFF], F32, kind="ExternalInput")
    b1r_d = nc.dram_tensor("b1r", [128, 128], F32, kind="ExternalInput")
    wof_d = nc.dram_tensor("wof", [IFF, D], F32, kind="ExternalInput")
    bof_d = nc.dram_tensor("bof", [1, D], F32, kind="ExternalInput")
    gam_d = nc.dram_tensor("gam", [1, D], F32, kind="ExternalInput")
    bet_d = nc.dram_tensor("bet", [1, D], F32, kind="ExternalInput")
    cos_d = nc.dram_tensor("cosr", [TC, 512], F32, kind="ExternalInput")
    sin_d = nc.dram_tensor("sinr", [TC, 512], F32, kind="ExternalInput")
    idf_d = nc.dram_tensor("idf", [128, 128], F32, kind="ExternalInput")

    out_d = nc.dram_tensor("out", [TC, D], F32, kind="ExternalOutput")

    ccKTin = nc.dram_tensor("ccKTin", [D, TC], BF16, kind="Internal")
    ccVin = nc.dram_tensor("ccVin", [TC, D], BF16, kind="Internal")
    ccKTout = nc.dram_tensor("ccKTout", [4 * D, TC], BF16, kind="Internal")
    ccVout = nc.dram_tensor("ccVout", [4 * TC, D], BF16, kind="Internal")
    part_d = nc.dram_tensor("part", [TC, D], F32, kind="Internal")
    qT_d = nc.dram_tensor("qTd", [D, TC], BF16, kind="Internal")

    dbg = {}
    if KDEV:
        dbg["xnT"] = nc.dram_tensor("dbg_xnT", [128, NKT * TC], BF16, kind="ExternalOutput")
        dbg["qT"] = nc.dram_tensor("dbg_qT", [128, NKT * TC], BF16, kind="ExternalOutput")
        dbg["oh"] = nc.dram_tensor("dbg_oh", [1, NE * TC], BF16, kind="ExternalOutput")
        dbg["lg"] = nc.dram_tensor("dbg_lg", [128, NE * NT], F32, kind="ExternalOutput")
        dbg["int"] = nc.dram_tensor("dbg_int", [128, NIC * TCH], BF16, kind="ExternalOutput")
        dbg["msk"] = nc.dram_tensor("dbg_msk", [128, NE * TCH], U8, kind="ExternalOutput")
        dbg["part"] = nc.dram_tensor("dbg_part", [TC, D], F32, kind="ExternalOutput")
        dbg["attnT"] = nc.dram_tensor("dbg_attnT", [128, NKT * TC], BF16, kind="ExternalOutput")

    sems = {}

    def S(name):
        return sems[name]

    with ExitStack() as ST:
        E = ST.enter_context
        for n in ["sLD", "sWG", "sBC", "sBCe", "sX", "sSUM", "sMEAN", "sSSQ",
                  "sSD", "sLNd", "sTm", "sTEv", "sRm", "sOHr", "sOHm", "sOHe",
                  "sQKV", "sVE", "sVD", "sQSS", "sQSD", "sRPd", "sQKTm",
                  "sQKe", "sKV", "sCC", "sW1L", "sMm", "sMe", "sHm", "sGe",
                  "sSe", "sWOL", "sYm", "sYm8", "sE1", "sE2", "sEp", "sPA",
                  "sXR", "sKL", "sQD", "sSm", "sEx", "sPV", "sDv1", "sDv2",
                  "sDv3", "sAT", "sWOA", "sYa", "sFin", "sOut", "sDBG",
                  "sXa", "sXb", "sVDa", "sVDb", "w1s0", "w1s1", "w1s2", "w1s3",
                  "sWOa", "sWOb", "sKLa", "sKLb", "sOutA", "sOutB"]:
            sems[n] = E(nc.semaphore(n))

        xnT = E(nc.sbuf_tensor("xnT", [128, NKT * TC], BF16))
        mrstd = E(nc.sbuf_tensor("mrstd", [128, 2 * NT], F32))
        gam_bc = E(nc.sbuf_tensor("gam_bc", [128, D], F32))
        bet_bc = E(nc.sbuf_tensor("bet_bc", [128, D], F32))
        bof_bc = E(nc.sbuf_tensor("bof_bc", [128, D], F32))
        boa_bc = E(nc.sbuf_tensor("boa_bc", [128, D], F32))
        ohrow = E(nc.sbuf_tensor("ohrow", [1, NE * TC], BF16))
        b1r_sb = E(nc.sbuf_tensor("b1r_sb", [128, 128], F32))
        wr_sb = E(nc.sbuf_tensor("wr_sb", [128, NKT * NE], F32))
        idf_sb = E(nc.sbuf_tensor("idf_sb", [128, 128], F32))
        idb_sb = E(nc.sbuf_tensor("idb_sb", [128, 128], BF16))
        ones_f = E(nc.sbuf_tensor("ones_f", [1, 128], F32))
        ones_b = E(nc.sbuf_tensor("ones_b", [1, 128], BF16))
        ones_r = E(nc.sbuf_tensor("ones_r", [1, 128], F32R))
        rowbuf = E(nc.sbuf_tensor("rowbuf", [1, D], F32))
        epsc = E(nc.sbuf_tensor("epsc", [128, 4], F32))

        with nc.Block() as blk:
            @blk.sync
            def _(sync):
                sync.dma_start(idf_sb[:], idf_d.ap()).then_inc(S("sLD"), 16)
                sync.dma_start(b1r_sb[:], b1r_d.ap()).then_inc(S("sLD"), 16)
                sync.dma_start(wr_sb[:].rearrange("p (k n) -> p k n", k=NKT),
                               wr_d.ap().rearrange("(k p) n -> p k n", p=128)).then_inc(S("sLD"), 16)
                sync.wait_ge(S("sLD"), 48)

            @blk.vector
            def _(v):
                v.wait_ge(S("sLD"), 48)
                v.memset(ones_f[:], 1.0)
                v.memset(ones_b[:], 1.0)
                with nc.allow_low_precision(reason="f32r ones helper"):
                    v.tensor_copy(ones_r[:], ones_f[:])
                v.memset(epsc[:, 0:1], 1e-5)
                v.memset(epsc[:, 1:2], 1e-6)
                v.memset(epsc[:, 2:3], 1e-12)
                v.tensor_copy(idb_sb[:], idf_sb[:])

        with ExitStack() as S0:
            ps_bc = S0.enter_context(nc.psum_tensor("ps_bc", [128, 512], F32))
            with nc.Block() as blk:
                @blk.sync
                def _(sync):
                    for i, dr in enumerate([gam_d, bet_d, bof_d, boa_d]):
                        if i >= 1:
                            sync.wait_ge(S("sBC"), 2 * i)
                        sync.dma_start(rowbuf[:], dr.ap()).then_inc(S("sLD"), 16)

                @blk.tensor
                def _(t):
                    for i in range(4):
                        t.wait_ge(S("sLD"), 48 + (i + 1) * 16)
                        for n in range(2):
                            m = 2 * i + n
                            if m >= 1:
                                t.wait_ge(S("sBCe"), m)
                            mm = t.matmul(ps_bc[:], ones_f[:, 0:128],
                                          rowbuf[0:1, n * 512:(n + 1) * 512],
                                          start=True, stop=True)
                            mm.then_inc(S("sBC"), 1)

                @blk.vector
                def _(v):
                    for i, dst in enumerate([gam_bc, bet_bc, bof_bc, boa_bc]):
                        for n in range(2):
                            m = 2 * i + n
                            v.wait_ge(S("sBC"), m + 1)
                            cp = v.tensor_copy(dst[:, n * 512:(n + 1) * 512], ps_bc[:])
                            cp.then_inc(S("sBCe"), 1)

        with ExitStack() as S1:
            E1 = S1.enter_context
            wqkv_sb = E1(nc.sbuf_tensor("wqkv_sb", [128, NKT * H3], BF16))
            xt = [E1(nc.sbuf_tensor(f"xt{i}", [128, D], F32)) for i in range(2)]
            cos_sb = [E1(nc.sbuf_tensor(f"cosb{i}", [128, 512], F32)) for i in range(2)]
            sin_sb = [E1(nc.sbuf_tensor(f"sinb{i}", [128, 512], F32)) for i in range(2)]
            stats = E1(nc.sbuf_tensor("stats", [128, 8], F32))
            xc_t = E1(nc.sbuf_tensor("xc_t", [128, D], F32))
            sq_t = E1(nc.sbuf_tensor("sq_t", [128, D], F32))
            xnf = E1(nc.sbuf_tensor("xnf", [128, D], F32))
            xnTw = E1(nc.sbuf_tensor("xnTw", [128, NKT * TCH], F32))
            logt = E1(nc.sbuf_tensor("logt", [128, NE], F32))
            ohbf = E1(nc.sbuf_tensor("ohbf", [128, NE], BF16))
            q_ro = E1(nc.sbuf_tensor("q_ro", [128, D], F32))
            k_ro = E1(nc.sbuf_tensor("k_ro", [128, D], F32))
            rt_a = E1(nc.sbuf_tensor("rt_a", [128, D], F32))
            ss_q = E1(nc.sbuf_tensor("ss_q", [128, 3 * NH], F32))
            ss_k = E1(nc.sbuf_tensor("ss_k", [128, 3 * NH], F32))
            q_bf = E1(nc.sbuf_tensor("q_bf", [128, D], BF16))
            k_bf = E1(nc.sbuf_tensor("k_bf", [128, D], BF16))
            v_bf = [E1(nc.sbuf_tensor(f"v_bf{i}", [128, D], BF16)) for i in range(2)]
            ktst = E1(nc.sbuf_tensor("ktst", [128, NKT * TCH], BF16))
            lgst = E1(nc.sbuf_tensor("lgst", [128, NE * NT], F32)) if KDEV == 1 else None
            qtst = E1(nc.sbuf_tensor("qtst", [128, NKT * TCH], BF16))

            qkv_ps = E1(nc.psum_tensor("qkv_ps", [128, H3], F32))
            ps_m = E1(nc.psum_tensor("ps_m", [128, 512], F32))
            ps_mb = E1(nc.psum_tensor("ps_mb", [128, 1024], BF16))

            with nc.Block() as blk:
                @blk.gpsimd
                def _(g_):
                    g_.dma_start(
                        wqkv_sb[:].rearrange("p (k n) -> p k n", k=NKT),
                        wqkv_d.ap().rearrange("(k p) n -> p k n", p=128),
                    ).then_inc(S("sWG"), 16)

                @blk.sync
                def _(sync):
                    for g in range(NT):
                        r0 = g * 128
                        if g >= 2:
                            sync.wait_ge(S("sLNd"), g - 1)
                            sync.wait_ge(S("sRPd"), g - 1)
                        slot = S("sXa") if g % 2 == 0 else S("sXb")
                        sync.dma_start(xt[g % 2][:], x_d.ap()[r0:r0 + 128, :]).then_inc(slot, 16)
                        sync.dma_start(cos_sb[g % 2][:], cos_d.ap()[r0:r0 + 128, :]).then_inc(slot, 16)
                        sync.dma_start(sin_sb[g % 2][:], sin_d.ap()[r0:r0 + 128, :]).then_inc(slot, 16)
                        if g >= 1:
                            gv = g - 1
                            sync.wait_ge(S("sVE"), gv + 1)
                            sync.dma_start(ccVin.ap()[gv * 128:gv * 128 + 128, :],
                                           v_bf[gv % 2][:]).then_inc(S("sVDa") if gv % 2 == 0 else S("sVDb"), 16)
                        if g % NTT == 0 and g >= NTT:
                            ch = g // NTT - 1
                            sync.wait_ge(S("sQKe"), (ch + 1) * NTT * 2 * NKT)
                            for j in range(NKT):
                                sync.dma_start(
                                    ccKTin.ap()[j * 128:(j + 1) * 128, ch * TCH:(ch + 1) * TCH],
                                    ktst[:, j * TCH:(j + 1) * TCH]).then_inc(S("sKV"), 16)
                                sync.dma_start(
                                    qT_d.ap()[j * 128:(j + 1) * 128, ch * TCH:(ch + 1) * TCH],
                                    qtst[:, j * TCH:(j + 1) * TCH]).then_inc(S("sKV"), 16)
                    gv = NT - 1
                    sync.wait_ge(S("sVE"), gv + 1)
                    sync.dma_start(ccVin.ap()[gv * 128:gv * 128 + 128, :],
                                   v_bf[gv % 2][:]).then_inc(S("sVDa") if gv % 2 == 0 else S("sVDb"), 16)
                    ch = NCH - 1
                    sync.wait_ge(S("sQKe"), (ch + 1) * NTT * 2 * NKT)
                    for j in range(NKT):
                        sync.dma_start(
                            ccKTin.ap()[j * 128:(j + 1) * 128, ch * TCH:(ch + 1) * TCH],
                            ktst[:, j * TCH:(j + 1) * TCH]).then_inc(S("sKV"), 16)
                        sync.dma_start(
                            qT_d.ap()[j * 128:(j + 1) * 128, ch * TCH:(ch + 1) * TCH],
                            qtst[:, j * TCH:(j + 1) * TCH]).then_inc(S("sKV"), 16)

                @blk.vector
                def _(v):
                    for g in range(NT):
                        ch, tt = g // NTT, g % NTT
                        xtile = xt[g % 2]
                        v.wait_ge(S("sXa") if g % 2 == 0 else S("sXb"), (g // 2 + 1) * 48)
                        v.reduce_sum(stats[:, 0:1], xtile[:], AX.X).then_inc(S("sSUM"), 1)
                        v.wait_ge(S("sMEAN"), g + 1)
                        v.tensor_scalar(xc_t[:], xtile[:], stats[:, 1:2], None, ALU.subtract)
                        v.tensor_mul(sq_t[:], xc_t[:], xc_t[:])
                        v.reduce_sum(stats[:, 2:3], sq_t[:], AX.X).then_inc(S("sSSQ"), 1)
                        v.wait_ge(S("sSD"), g + 1)
                        v.reciprocal(stats[:, 4:5], stats[:, 3:4])
                        v.engine_nop().then_inc(S("sE1"), 1)
                        v.wait_ge(S("sE1"), 3 * g + 1)
                        v.tensor_copy(mrstd[:, 2 * g:2 * g + 1], stats[:, 1:2])
                        v.tensor_copy(mrstd[:, 2 * g + 1:2 * g + 2], stats[:, 4:5])
                        if g >= 1:
                            v.wait_ge(S("sTm"), g * NKT)
                        ts = v.tensor_scalar(xnf[:], xtile[:], stats[:, 1:2], stats[:, 4:5],
                                             ALU.subtract, ALU.mult)
                        ts.then_inc(S("sLNd"), 1)
                        if g >= 4:
                            v.wait_ge(S("sRm"), g - 3)
                        for j in range(NKT):
                            m = g * NKT + j
                            v.wait_ge(S("sTm"), m + 1)
                            v.tensor_copy(xnT[:, j * TC + g * 128:j * TC + g * 128 + 128],
                                          ps_m[:, 0:128])
                            cp = v.tensor_copy(xnTw[:, j * TCH + tt * 128:j * TCH + tt * 128 + 128],
                                               ps_m[:, 0:128])
                            cp.then_inc(S("sTEv"), 1)
                        v.wait_ge(S("sRm"), g + 1)
                        if KDEV == 1:
                            v.tensor_copy(lgst[:, g * NE:(g + 1) * NE], ps_m[:, 0:NE])
                        v.reduce_max(stats[:, 5:6], ps_m[:, 0:NE], AX.X)
                        v.engine_nop().then_inc(S("sE1"), 1)
                        v.wait_ge(S("sE1"), 3 * g + 2)
                        ts = v.tensor_scalar(ohbf[:], ps_m[:, 0:NE], stats[:, 5:6], None, ALU.is_ge)
                        ts.then_inc(S("sOHr"), 1)
                        for e in range(NE):
                            m = g * NE + e
                            v.wait_ge(S("sOHm"), m + 1)
                            cp = v.tensor_copy(ohrow[0:1, e * TC + g * 128:e * TC + g * 128 + 128],
                                               ps_mb[0:1, 0:128])
                            cp.then_inc(S("sOHe"), 1)
                        v.wait_ge(S("sQKV"), g + 1)
                        if g >= 1:
                            v.wait_ge(S("sQKTm"), g * 2 * NKT)
                        cosv = cos_sb[g % 2][:].rearrange("p (h u) -> p h u", u=32)
                        sinv = sin_sb[g % 2][:].rearrange("p (h u) -> p h u", u=32)
                        if KDEV_NOROPE:
                            v.tensor_copy(q_ro[:], qkv_ps[:, 0:D])
                            v.tensor_copy(k_ro[:], qkv_ps[:, D:2 * D])
                        for (base, ro) in (() if KDEV_NOROPE else ((0, q_ro), (D, k_ro))):
                            pv = qkv_ps[:, base:base + D].rearrange("p (h z u) -> p h z u", z=2, u=32)
                            rv = ro[:].rearrange("p (h z u) -> p h z u", z=2, u=32)
                            tv = rt_a[:].rearrange("p (h z u) -> p h z u", z=2, u=32)
                            v.tensor_tensor(tv[:, :, 0], pv[:, :, 0], cosv, ALU.mult)
                            v.tensor_tensor(tv[:, :, 1], pv[:, :, 1], sinv, ALU.mult)
                            v.tensor_tensor(rv[:, :, 0], tv[:, :, 0], tv[:, :, 1], ALU.subtract)
                            v.tensor_tensor(tv[:, :, 0], pv[:, :, 0], sinv, ALU.mult)
                            v.tensor_tensor(tv[:, :, 1], pv[:, :, 1], cosv, ALU.mult)
                            v.tensor_tensor(rv[:, :, 1], tv[:, :, 0], tv[:, :, 1], ALU.add)
                        for (ro, ss) in ((q_ro, ss_q), (k_ro, ss_k)):
                            v.tensor_mul(sq_t[:], ro[:], ro[:])
                            r = v.reduce_sum(ss[:, 0:NH].rearrange("p (h o) -> p h o", o=1),
                                             sq_t[:].rearrange("p (h u) -> p h u", u=HD), AX.X)
                            r.then_inc(S("sQSS"), 1)
                        v.wait_ge(S("sQSD"), 2 * g + 2)
                        v.reciprocal(ss_q[:, 2 * NH:3 * NH], ss_q[:, NH:2 * NH])
                        v.reciprocal(ss_k[:, 2 * NH:3 * NH], ss_k[:, NH:2 * NH])
                        v.engine_nop().then_inc(S("sE1"), 1)
                        v.wait_ge(S("sE1"), 3 * g + 3)
                        for (ro, ss, obf) in ((q_ro, ss_q, q_bf), (k_ro, ss_k, k_bf)):
                            for h in range(NH):
                                v.tensor_scalar(obf[:, h * HD:(h + 1) * HD],
                                                ro[:, h * HD:(h + 1) * HD],
                                                ss[:, 2 * NH + h:2 * NH + h + 1], None, ALU.mult)
                        v.engine_nop().then_inc(S("sRPd"), 1)
                        if tt == 0 and ch >= 1:
                            v.wait_ge(S("sKV"), ch * 2 * NKT * 16)
                        for j in range(2 * NKT):
                            m = g * 2 * NKT + j
                            v.wait_ge(S("sQKTm"), m + 1)
                            if j < NKT:
                                cp = v.tensor_copy(qtst[:, j * TCH + tt * 128:j * TCH + tt * 128 + 128],
                                                   ps_mb[:, 0:128])
                            else:
                                jj = j - NKT
                                cp = v.tensor_copy(ktst[:, jj * TCH + tt * 128:jj * TCH + tt * 128 + 128],
                                                   ps_mb[:, 0:128])
                            cp.then_inc(S("sQKe"), 1)

                @blk.scalar
                def _(sc):
                    for g in range(NT):
                        sc.wait_ge(S("sSUM"), g + 1)
                        sc.mul(stats[:, 1:2], stats[:, 0:1], 1.0 / D).then_inc(S("sMEAN"), 1)
                        sc.wait_ge(S("sSSQ"), g + 1)
                        a = sc.activation(stats[:, 3:4], stats[:, 2:3], AF.Sqrt,
                                          bias=epsc[:, 0:1], scale=1.0 / D)
                        a.then_inc(S("sSD"), 1)
                        sc.wait_ge(S("sQSS"), 2 * g + 1)
                        a = sc.activation(ss_q[:, NH:2 * NH], ss_q[:, 0:NH], AF.Sqrt, bias=epsc[:, 1:2])
                        a.then_inc(S("sQSD"), 1)
                        sc.wait_ge(S("sQSS"), 2 * g + 2)
                        a = sc.activation(ss_k[:, NH:2 * NH], ss_k[:, 0:NH], AF.Sqrt, bias=epsc[:, 1:2])
                        a.then_inc(S("sQSD"), 1)
                        sc.wait_ge(S("sQKV"), g + 1)
                        if g >= 2:
                            sc.wait_ge(S("sVDa") if g % 2 == 0 else S("sVDb"), (g // 2) * 16)
                        a = sc.copy(v_bf[g % 2][:], qkv_ps[:, 2 * D:3 * D])
                        a.then_inc(S("sVE"), 1)

                @blk.tensor
                def _(t):
                    for g in range(NT):
                        ch, tt = g // NTT, g % NTT
                        t.wait_ge(S("sLNd"), g + 1)
                        if g >= 1:
                            t.wait_ge(S("sOHr"), g)
                        for j in range(NKT):
                            m = g * NKT + j
                            if m >= 1:
                                t.wait_ge(S("sTEv"), m)
                            tr = t.transpose(ps_m[:, 0:128], xnf[:, j * 128:(j + 1) * 128], idf_sb[:])
                            tr.then_inc(S("sTm"), 1)
                        t.wait_ge(S("sTEv"), (g + 1) * NKT)
                        for k in range(NKT):
                            mm = t.matmul(ps_m[:, 0:NE],
                                          xnTw[:, k * TCH + tt * 128:k * TCH + tt * 128 + 128],
                                          wr_sb[:].rearrange("p (k2 n) -> p k2 n", k2=NKT)[:, k, :],
                                          start=(k == 0), stop=(k == NKT - 1))
                        mm.then_inc(S("sRm"), 1)
                        t.wait_ge(S("sOHr"), g + 1)
                        for e in range(NE):
                            if e == 0:
                                if g >= 1:
                                    t.wait_ge(S("sQKe"), g * 2 * NKT)
                            else:
                                t.wait_ge(S("sOHe"), g * NE + e)
                            tr = t.transpose(ps_mb[0:1, 0:128], ohbf[:, e:e + 1], idb_sb[:])
                            tr.then_inc(S("sOHm"), 1)
                        if g == 0:
                            t.wait_ge(S("sWG"), 16)
                        t.wait_ge(S("sOHe"), (g + 1) * NE)
                        if g >= 1:
                            t.wait_ge(S("sRPd"), g)
                            t.wait_ge(S("sVE"), g)
                        for n in range(6):
                            for k in range(NKT):
                                mm = t.matmul(qkv_ps[:, n * 512:(n + 1) * 512],
                                              xnT[:, k * TC + g * 128:k * TC + g * 128 + 128],
                                              wqkv_sb[:, k * H3 + n * 512:k * H3 + (n + 1) * 512],
                                              start=(k == 0), stop=(k == NKT - 1))
                        mm.then_inc(S("sQKV"), 1)
                        t.wait_ge(S("sRPd"), g + 1)
                        for j in range(2 * NKT):
                            m = g * 2 * NKT + j
                            if j == 0:
                                t.wait_ge(S("sOHe"), (g + 1) * NE)
                            else:
                                t.wait_ge(S("sQKe"), m)
                            src = q_bf if j < NKT else k_bf
                            jj = j % NKT
                            tr = t.transpose(ps_mb[:, 0:128], src[:, jj * 128:(jj + 1) * 128], idb_sb[:])
                            tr.then_inc(S("sQKTm"), 1)

        with nc.Block() as blk:
            @blk.gpsimd
            def _(g_):
                g_.wait_ge(S("sVDa"), NT * 8)
                g_.wait_ge(S("sVDb"), NT * 8)
                g_.wait_ge(S("sKV"), NCH * 2 * NKT * 16)
                g_.collective_compute(
                    "AllGather", ALU.bypass,
                    replica_groups=[[0, 1, 2, 3], [4, 5, 6, 7]],
                    ins=[ccKTin.ap().opt()], outs=[ccKTout.ap().opt()],
                ).then_inc(S("sCC"), 1)
                g_.collective_compute(
                    "AllGather", ALU.bypass,
                    replica_groups=[[0, 1, 2, 3], [4, 5, 6, 7]],
                    ins=[ccVin.ap().opt()], outs=[ccVout.ap().opt()],
                ).then_inc(S("sCC"), 1)

        if KDEV == 1:
            with nc.Block() as blk:
                @blk.sync
                def _(sync):
                    sync.dma_start(dbg["xnT"].ap(), xnT[:]).then_inc(S("sDBG"), 16)
                    sync.dma_start(dbg["qT"].ap(), qT_d.ap()).then_inc(S("sDBG"), 16)
                    sync.dma_start(dbg["oh"].ap(), ohrow[:]).then_inc(S("sDBG"), 16)
                    sync.dma_start(dbg["lg"].ap(), lgst[:]).then_inc(S("sDBG"), 16)
                    sync.wait_ge(S("sDBG"), 64)

        if KDEV == 0 or KDEV >= 2:
            with ExitStack() as S2:
                E2 = S2.enter_context
                w1sb = [E2(nc.sbuf_tensor(f"w1sb{i}", [128, NE * NKT * 128], BF16)) for i in range(4)]
                wofsb = [E2(nc.sbuf_tensor(f"wofsb{i}", [128, 8 * D], BF16)) for i in range(2)]
                intT = E2(nc.sbuf_tensor("intT", [128, NIC * TCH], BF16))
                mskst = E2(nc.sbuf_tensor("mskst", [128, NE * TCH], U8)) if KDEV == 2 else None
                g_sb = [E2(nc.sbuf_tensor(f"g_sb{i}", [128, TCH], BF16)) for i in range(NE)]
                seltmp = [E2(nc.sbuf_tensor(f"seltmp{i}", [128, TCH], BF16)) for i in range(2)]
                mask8 = [E2(nc.sbuf_tensor(f"mask8{i}", [128, TCH], U8)) for i in range(NE)]
                xr = [E2(nc.sbuf_tensor(f"xr{i}", [128, D], F32)) for i in range(2)]
                epi1 = E2(nc.sbuf_tensor("epi1", [128, D], F32))
                epi2 = E2(nc.sbuf_tensor("epi2", [128, D], F32))
                est = E2(nc.sbuf_tensor("est", [128, 8], F32))
                ptile = E2(nc.sbuf_tensor("ptile", [128, D], F32))

                for ch in range(NCH):
                    c0 = ch * TCH
                    with ExitStack() as SA:
                        hps = [SA.enter_context(nc.psum_tensor(f"hps{ch}_{i}", [128, TCH], F32))
                               for i in range(4)]
                        mps = SA.enter_context(nc.psum_tensor(f"mps{ch}", [128, TCH], F32))
                        with nc.Block() as blk:
                            @blk.gpsimd
                            def _(g_):
                                for grp in range(NIC):
                                    gg = ch * NIC + grp
                                    if gg >= 4:
                                        g_.wait_ge(S("sHm"), (gg - 3) * 4)
                                    for e in range(NE):
                                        g_.dma_start(
                                            w1sb[gg % 4][:].rearrange("p (e2 k n) -> p e2 k n",
                                                                      e2=NE, k=NKT)[:, e],
                                            w1_d.ap()[e].rearrange("(k p) i -> p k i", p=128)
                                            [:, :, grp * 128:(grp + 1) * 128],
                                        ).then_inc(S(f"w1s{gg % 4}"), 16)

                            @blk.tensor
                            def _(t):
                                for e in range(NE):
                                    if ch * NE + e >= 1:
                                        t.wait_ge(S("sMe"), ch * NE + e)
                                    mm = t.matmul(mps[:], ones_b[:, 0:128],
                                                  ohrow[0:1, e * TC + c0:e * TC + c0 + TCH],
                                                  start=True, stop=True)
                                    mm.then_inc(S("sMm"), 1)
                                for i in range(NIC):
                                    for e in range(NE):
                                        ie = ch * NIC * NE + i * NE + e
                                        gg2 = ch * NIC + i
                                        t.wait_ge(S(f"w1s{gg2 % 4}"), (gg2 // 4 + 1) * 64)
                                        if ie >= 4:
                                            t.wait_ge(S("sGe"), ie - 3)
                                        bv = w1sb[(ch * NIC + i) % 4][:].rearrange(
                                            "p (e2 k n) -> p e2 k n", e2=NE, k=NKT)
                                        for k in range(NKT):
                                            mm = t.matmul(
                                                hps[(i * NE + e) % 4][:],
                                                bv[:, e, k, :],
                                                xnT[:, k * TC + c0:k * TC + c0 + TCH],
                                                start=(k == 0), stop=(k == NKT - 1))
                                        mm.then_inc(S("sHm"), 1)

                            @blk.vector
                            def _(v):
                                for e in range(NE):
                                    v.wait_ge(S("sMm"), ch * NE + e + 1)
                                    ts = v.tensor_single_scalar(mask8[e][:], mps[:], 0.5, ALU.is_gt)
                                    ts.then_inc(S("sMe"), 1)
                                if KDEV == 2 and ch == 0:
                                    for e in range(NE):
                                        v.tensor_copy(mskst[:, e * TCH:(e + 1) * TCH], mask8[e][:])
                                    v.engine_nop().then_inc(S("sDBG"), 1)
                                for i in range(NIC):
                                    v.wait_ge(S("sGe"), ch * NIC * NE + i * NE + NE)
                                    v.select(seltmp[0][:], mask8[2][:], g_sb[2][:], g_sb[3][:])
                                    v.select(seltmp[1][:], mask8[1][:], g_sb[1][:], seltmp[0][:])
                                    s3 = v.select(intT[:, i * TCH:(i + 1) * TCH],
                                                  mask8[0][:], g_sb[0][:], seltmp[1][:])
                                    s3.then_inc(S("sSe"), 1)

                            @blk.scalar
                            def _(sc):
                                for i in range(NIC):
                                    for e in range(NE):
                                        ie = ch * NIC * NE + i * NE + e
                                        sc.wait_ge(S("sHm"), ie + 1)
                                        if ch * NIC + i >= 1:
                                            sc.wait_ge(S("sSe"), ch * NIC + i)
                                        a = sc.activation(g_sb[e][:], hps[(i * NE + e) % 4][:],
                                                          AF.Gelu,
                                                          bias=b1r_sb[:, e * 32 + i:e * 32 + i + 1])
                                        a.then_inc(S("sGe"), 1)

                    if KDEV == 2 and ch == 0:
                        with nc.Block() as blk:
                            @blk.sync
                            def _(sync):
                                sync.wait_ge(S("sSe"), NIC)
                                sync.dma_start(dbg["int"].ap(), intT[:]).then_inc(S("sDBG"), 16)
                    if not KDEV_B2B:
                        if ch < NCH - 1:
                            continue
                        with nc.Block() as blk:
                            @blk.sync
                            def _(sync):
                                sync.wait_ge(S("sSe"), NCH * NIC)
                                sync.dma_start(dbg["int"].ap(), intT[:]).then_inc(S("sDBG"), 16)
                            @blk.gpsimd
                            def _(g_):
                                if os.environ.get("KDEV_CCWAIT", "1") == "1":
                                    g_.wait_ge(S("sCC"), 2)
                        continue
                    with ExitStack() as SB:
                        yps = [SB.enter_context(nc.psum_tensor(f"yps{ch}_{i}", [128, D], F32))
                               for i in range(NTT)]
                        with nc.Block() as blk:
                            @blk.gpsimd
                            def _(g_):
                                for grp in range(4):
                                    gg = ch * 4 + grp
                                    if gg >= 2:
                                        g_.wait_ge(S("sYm8"), gg - 1)
                                    g_.dma_start(
                                        wofsb[grp % 2][:].rearrange("p (i n) -> p i n", i=8),
                                        wof_d.ap().rearrange("(i p) n -> p i n", p=128)
                                        [:, grp * 8:(grp + 1) * 8, :],
                                    ).then_inc(S("sWOa") if gg % 2 == 0 else S("sWOb"), 16)

                            @blk.tensor
                            def _(t):
                                for i in range(NIC):
                                    gg3 = ch * 4 + i // 8
                                    t.wait_ge(S("sWOa") if gg3 % 2 == 0 else S("sWOb"), (gg3 // 2 + 1) * 16)
                                    for tt in range(NTT):
                                        for n in range(2):
                                            mm = t.matmul(
                                                yps[tt][:, n * 512:(n + 1) * 512],
                                                intT[:, i * TCH + tt * 128:i * TCH + tt * 128 + 128],
                                                wofsb[(i // 8) % 2][:, (i % 8) * D + n * 512:
                                                                    (i % 8) * D + (n + 1) * 512],
                                                start=(i == 0), stop=(i == NIC - 1))
                                    if i % 8 == 7:
                                        mm.then_inc(S("sYm8"), 1)

                            @blk.sync
                            def _(sync):
                                for tt in range(NTT):
                                    g = ch * NTT + tt
                                    if tt >= 2:
                                        sync.wait_ge(S("sEp"), g - 1)
                                    sync.dma_start(xr[tt % 2][:],
                                                   x_d.ap()[g * 128:g * 128 + 128, :]).then_inc(S("sXR"), 16)
                                    if tt >= 1:
                                        sync.wait_ge(S("sEp"), g)
                                        sync.dma_start(part_d.ap()[(g - 1) * 128:g * 128, :],
                                                       ptile[:]).then_inc(S("sPA"), 16)
                                g = ch * NTT + NTT - 1
                                sync.wait_ge(S("sEp"), g + 1)
                                sync.dma_start(part_d.ap()[g * 128:(g + 1) * 128, :],
                                               ptile[:]).then_inc(S("sPA"), 16)

                            @blk.vector
                            def _(v):
                                for tt in range(NTT):
                                    g = ch * NTT + tt
                                    v.wait_ge(S("sYm8"), 4 * (ch + 1))
                                    v.wait_ge(S("sXR"), (g + 1) * 16)
                                    v.wait_ge(S("sPA"), g * 16)
                                    v.tensor_scalar(epi1[:], xr[tt % 2][:], mrstd[:, 2 * g:2 * g + 1],
                                                    mrstd[:, 2 * g + 1:2 * g + 2],
                                                    ALU.subtract, ALU.mult)
                                    v.tensor_tensor(epi2[:], yps[tt][:], bof_bc[:], ALU.add)
                                    v.tensor_tensor(epi1[:], epi1[:], epi2[:], ALU.add)
                                    v.reduce_sum(est[:, 0:1], epi1[:], AX.X).then_inc(S("sE1"), 1)
                                    v.wait_ge(S("sE2"), 2 * g + 1)
                                    v.tensor_scalar(epi2[:], epi1[:], est[:, 1:2], None, ALU.subtract)
                                    v.tensor_mul(epi1[:], epi2[:], epi2[:])
                                    v.reduce_sum(est[:, 2:3], epi1[:], AX.X).then_inc(S("sE1"), 1)
                                    v.wait_ge(S("sE2"), 2 * g + 2)
                                    v.reciprocal(est[:, 4:5], est[:, 3:4])
                                    v.engine_nop().then_inc(S("sE1"), 1)
                                    v.wait_ge(S("sE1"), 3 * NT + 3 * g + 3)
                                    v.scalar_tensor_tensor(epi1[:], epi2[:], est[:, 4:5], gam_bc[:],
                                                           ALU.mult, ALU.mult)
                                    v.tensor_tensor(epi1[:], epi1[:], bet_bc[:], ALU.add)
                                    ad = v.tensor_tensor(ptile[:], epi1[:], xr[tt % 2][:], ALU.add)
                                    ad.then_inc(S("sEp"), 1)

                            @blk.scalar
                            def _(sc):
                                for tt in range(NTT):
                                    g = ch * NTT + tt
                                    sc.wait_ge(S("sE1"), 3 * NT + 3 * g + 1)
                                    sc.mul(est[:, 1:2], est[:, 0:1], 1.0 / D).then_inc(S("sE2"), 1)
                                    sc.wait_ge(S("sE1"), 3 * NT + 3 * g + 2)
                                    a = sc.activation(est[:, 3:4], est[:, 2:3], AF.Sqrt,
                                                      bias=epsc[:, 2:3], scale=1.0 / D)
                                    a.then_inc(S("sE2"), 1)

            if KDEV == 2 and KDEV_B2B:
                with nc.Block() as blk:
                    @blk.gpsimd
                    def _(g_):
                        g_.wait_ge(S("sPA"), NT * 16)
                        g_.dma_start(dbg["part"].ap(), part_d.ap()).then_inc(S("sDBG"), 16)
                        g_.wait_ge(S("sDBG"), 17)
                        g_.dma_start(dbg["msk"].ap(), mskst[:]).then_inc(S("sDBG"), 16)
                        g_.dma_start(dbg["oh"].ap(), ohrow[:]).then_inc(S("sDBG"), 16)
                        g_.wait_ge(S("sDBG"), 49)

        if KDEV == 0 or KDEV >= 3:
            with ExitStack() as S3:
                E3 = S3.enter_context
                attnT = E3(nc.sbuf_tensor("attnT", [128, NKT * TC], BF16))
                kt2 = [E3(nc.sbuf_tensor(f"kt2_{i}", [128, 32 * 128], BF16)) for i in range(2)]
                vt = [E3(nc.sbuf_tensor(f"vt_{i}", [128, 64 * 65], BF16)) for i in range(2)]
                qdup = [E3(nc.sbuf_tensor(f"qdup{i}", [128, TC], BF16)) for i in range(2)]
                pT = [E3(nc.sbuf_tensor(f"pT{i}", [128, 1024], BF16)) for i in range(2)]
                rd_r = E3(nc.sbuf_tensor("rd_r", [1, 512], F32R))
                rdsb = E3(nc.sbuf_tensor("rdsb", [64, 512], F32))
                woasb = E3(nc.sbuf_tensor("woasb", [128, NKT * D], BF16))
                prt = [E3(nc.sbuf_tensor(f"prt{i}", [128, D], F32)) for i in range(2)]
                fin = [E3(nc.sbuf_tensor(f"fin{i}", [128, D], F32)) for i in range(2)]

                SH = ExitStack()
                sps = [SH.enter_context(nc.psum_tensor(f"sps{i}", [128, 1024], F32)) for i in range(2)]
                ovps = SH.enter_context(nc.psum_tensor("ovps", [65, 512], F32))
                rdps = SH.enter_context(nc.psum_tensor("rdps", [64, 512], F32))

                def head_loads(sync, h):
                    j, half = h // 2, (h % 2) * 64
                    slot = S("sKLa") if h % 2 == 0 else S("sKLb")
                    sync.dma_start(qdup[h % 2][0:64, :],
                                   qT_d.ap()[h * 64:h * 64 + 64, :]).then_inc(slot, 16)
                    sync.dma_start(qdup[h % 2][64:128, :],
                                   qT_d.ap()[h * 64:h * 64 + 64, :]).then_inc(slot, 16)
                    for r in range(4):
                        for par in range(2):
                            src = ccKTout.ap()[r * D + j * 128 + half:r * D + j * 128 + half + 64, :] \
                                .rearrange("d (m t2) -> d m t2", t2=128)[:, par::2, :]
                            dst = kt2[h % 2][par * 64:par * 64 + 64, r * 1024:(r + 1) * 1024] \
                                .rearrange("d (m t2) -> d m t2", t2=128)
                            sync.dma_start(dst, src).then_inc(slot, 16)
                    for r in range(4):
                        src = ccVout.ap()[r * TC:(r + 1) * TC, h * HD:(h + 1) * HD] \
                            .rearrange("(m p) d -> p m d", p=128)
                        dst = vt[h % 2][:].rearrange("p (m c) -> p m c", c=65)[:, 16 * r:16 * (r + 1), 0:64]
                        sync.dma_start(dst, src).then_inc(slot, 16)

                with nc.Block() as blk:
                    @blk.vector
                    def _(v):
                        for i in range(2):
                            v.memset(vt[i][:].rearrange("p (m c) -> p m c", c=65)[:, :, 64:65], 1.0)

                    @blk.gpsimd
                    def _(g_):
                        g_.dma_start(
                            woasb[:].rearrange("p (k n) -> p k n", k=NKT),
                            woa_d.ap().rearrange("(k p) n -> p k n", p=128),
                        ).then_inc(S("sWOA"), 16)

                for h in range(NH):
                    with nc.Block() as blk:
                        @blk.sync
                        def _(sync):
                            if h == 0:
                                sync.wait_ge(S("sCC"), 2)
                                head_loads(sync, 0)
                                head_loads(sync, 1)
                            elif h < NH - 1:
                                if h >= 1:
                                    sync.wait_ge(S("sPV"), 2 * 128 * h)
                                head_loads(sync, h + 1)

                        @blk.tensor
                        def _(t):
                            t.wait_ge(S("sKLa") if h % 2 == 0 else S("sKLb"), (h // 2 + 1) * 14 * 16)
                            for tc in range(4):
                                if h * 4 + tc >= 1:
                                    t.wait_ge(S("sAT"), h * 4 + tc)
                                    t.wait_ge(S("sDv3"), h * 4 + tc)
                                for m in range(32):
                                    gp = h * 128 + tc * 32 + m
                                    pb = m % 2
                                    if gp >= 2:
                                        t.wait_ge(S("sEx"), gp - 1)
                                    t.matmul(sps[pb][:, 0:512],
                                             kt2[h % 2][0:64, m * 128:(m + 1) * 128],
                                             qdup[h % 2][0:64, tc * 512:(tc + 1) * 512],
                                             start=True, stop=True, tile_position=(0, 0))
                                    mm2 = t.matmul(sps[pb][:, 512:1024],
                                                   kt2[h % 2][64:128, m * 128:(m + 1) * 128],
                                                   qdup[h % 2][64:128, tc * 512:(tc + 1) * 512],
                                                   start=True, stop=True, tile_position=(64, 0))
                                    mm2.then_inc(S("sSm"), 1)
                                    t.wait_ge(S("sEx"), gp + 1)
                                    vv = vt[h % 2][:].rearrange("p (m2 c) -> p m2 c", c=65)
                                    mm = t.matmul(ovps[:], vv[:, 2 * m, :], pT[pb][:, 0:512],
                                                  start=(m == 0), stop=False)
                                    mm.then_inc(S("sPV"), 1)
                                    mm = t.matmul(ovps[:], vv[:, 2 * m + 1, :], pT[pb][:, 512:1024],
                                                  start=False, stop=(m == 31))
                                    mm.then_inc(S("sPV"), 1)
                                t.wait_ge(S("sDv1"), h * 4 + tc + 1)
                                mm = t.matmul(rdps[:], ones_r[0:1, 0:64], rd_r[0:1, :],
                                              start=True, stop=True)
                                mm.then_inc(S("sDv2"), 1)

                        @blk.scalar
                        def _(sc):
                            for tc in range(4):
                                hc = h * 4 + tc
                                for m in range(32):
                                    gp = h * 128 + tc * 32 + m
                                    pb = m % 2
                                    sc.wait_ge(S("sSm"), gp + 1)
                                    if gp >= 2:
                                        sc.wait_ge(S("sPV"), 2 * (gp - 1))
                                    a = sc.activation(pT[pb][:], sps[pb][:], AF.Exp, scale=0.125)
                                    a.then_inc(S("sEx"), 1)
                                sc.wait_ge(S("sDv2"), hc + 1)
                                if hc >= 1:
                                    sc.wait_ge(S("sAT"), hc)
                                a = sc.copy(rdsb[:], rdps[:])
                                a.then_inc(S("sDv3"), 1)

                        @blk.vector
                        def _(v):
                            jj, half = h // 2, (h % 2) * 64
                            for tc in range(4):
                                hc = h * 4 + tc
                                v.wait_ge(S("sPV"), h * 256 + (tc + 1) * 64)
                                with nc.allow_low_precision(reason="f32r bcast row for denom (1.2e-4 rel, fine)"):
                                    v.reciprocal(rd_r[0:1, :], ovps[64:65, :])
                                v.engine_nop().then_inc(S("sDv1"), 1)
                                v.wait_ge(S("sDv3"), hc + 1)
                                mu = v.tensor_tensor(
                                    attnT[half:half + 64, jj * TC + tc * 512:jj * TC + (tc + 1) * 512],
                                    ovps[0:64, :], rdsb[:], ALU.mult)
                                mu.then_inc(S("sAT"), 1)

                SH.close()
                with ExitStack() as SZ:
                    yaps = [SZ.enter_context(nc.psum_tensor(f"yaps{i}", [128, D], F32))
                            for i in range(2)]
                    with nc.Block() as blk:
                        @blk.tensor
                        def _(t):
                            t.wait_ge(S("sWOA"), 16)
                            for g in range(NT):
                                if g >= 2:
                                    t.wait_ge(S("sFin"), g - 1)
                                for k in range(NKT):
                                    for n in range(2):
                                        mm = t.matmul(yaps[g % 2][:, n * 512:(n + 1) * 512],
                                                      attnT[:, k * TC + g * 128:k * TC + g * 128 + 128],
                                                      woasb[:, k * D + n * 512:k * D + (n + 1) * 512],
                                                      start=(k == 0), stop=(k == NKT - 1))
                                mm.then_inc(S("sYa"), 1)

                        @blk.sync
                        def _(sync):
                            for g in range(NT):
                                if g == 0:
                                    sync.wait_ge(S("sPA"), NT * 16)
                                if g >= 2:
                                    sync.wait_ge(S("sFin"), g - 1)
                                sync.dma_start(prt[g % 2][:],
                                               part_d.ap()[g * 128:g * 128 + 128, :]).then_inc(S("sXR"), 16)
                                if g >= 1:
                                    sync.wait_ge(S("sFin"), g)
                                    sync.dma_start(out_d.ap()[(g - 1) * 128:g * 128, :],
                                                   fin[(g - 1) % 2][:]).then_inc(
                                        S("sOutA") if (g - 1) % 2 == 0 else S("sOutB"), 16)
                            g = NT - 1
                            sync.wait_ge(S("sFin"), g + 1)
                            sync.dma_start(out_d.ap()[g * 128:(g + 1) * 128, :],
                                           fin[g % 2][:]).then_inc(S("sOutB"), 16)
                            if KDEV >= 3:
                                sync.dma_start(dbg["attnT"].ap(), attnT[:]).then_inc(S("sOutB"), 16)
                                sync.wait_ge(S("sOutB"), NT * 8 + 16)
                            else:
                                sync.wait_ge(S("sOutB"), NT * 8)
                            sync.wait_ge(S("sOutA"), NT * 8)

                        @blk.vector
                        def _(v):
                            for g in range(NT):
                                v.wait_ge(S("sYa"), g + 1)
                                v.wait_ge(S("sXR"), (NT + g + 1) * 16)
                                if g >= 2:
                                    v.wait_ge(S("sOutA") if g % 2 == 0 else S("sOutB"),
                                              (g // 2) * 16)
                                v.tensor_tensor(fin[g % 2][:], yaps[g % 2][:], boa_bc[:], ALU.add)
                                ad = v.tensor_tensor(fin[g % 2][:], fin[g % 2][:], prt[g % 2][:], ALU.add)
                                ad.then_inc(S("sFin"), 1)

    nc.finalize()
    return nc


_NC_CACHE = {}


def kernel(L0_attn_x, L0_attn_Wqkv, L0_attn_Wo, L0_attn_bo, L0_ffn_W_router,
           L0_ffn_W1, L0_ffn_b1, L0_ffn_Wo, L0_ffn_bo, L0_ffn_ln_gamma,
           L0_ffn_ln_beta):
    x = np.ascontiguousarray(np.asarray(L0_attn_x, np.float32).reshape(NCORE * TC, D))
    wqkv = np.ascontiguousarray(np.asarray(L0_attn_Wqkv, np.float32))
    woa = np.ascontiguousarray(np.asarray(L0_attn_Wo, np.float32))
    boa = np.asarray(L0_attn_bo, np.float32).reshape(1, D)
    wr = np.ascontiguousarray(np.asarray(L0_ffn_W_router, np.float32))
    w1 = np.ascontiguousarray(np.asarray(L0_ffn_W1, np.float32))
    b1 = np.asarray(L0_ffn_b1, np.float32)
    b1r = np.ascontiguousarray(b1.reshape(NE, 32, 128).transpose(2, 0, 1).reshape(128, 128))
    wof = np.ascontiguousarray(np.asarray(L0_ffn_Wo, np.float32))
    bof = np.asarray(L0_ffn_bo, np.float32).reshape(1, D)
    gam = np.asarray(L0_ffn_ln_gamma, np.float32).reshape(1, D)
    bet = np.asarray(L0_ffn_ln_beta, np.float32).reshape(1, D)
    idf = np.eye(128, dtype=np.float32)

    inv = 1.0 / (10000.0 ** (np.arange(32, dtype=np.float64) / 32.0))
    in_maps = []
    for c in range(NCORE):
        pos = np.arange(TC, dtype=np.float64) + (c % 4) * TC
        ang = pos[:, None] * inv[None, :]
        cosr = np.ascontiguousarray(np.tile(np.cos(ang), (1, NH)).astype(np.float32))
        sinr = np.ascontiguousarray(np.tile(np.sin(ang), (1, NH)).astype(np.float32))
        in_maps.append({
            "x": x[c * TC:(c + 1) * TC], "wqkv": wqkv, "woa": woa, "boa": boa,
            "wr": wr, "w1": w1, "b1r": b1r, "wof": wof, "bof": bof,
            "gam": gam, "bet": bet, "cosr": cosr, "sinr": sinr, "idf": idf,
        })

    if KDEV not in _NC_CACHE:
        _NC_CACHE[KDEV] = build()
    nc = _NC_CACHE[KDEV]
    res = run_bass_kernel_spmd(nc, in_maps, core_ids=list(range(NCORE)))
    kernel._res = res
    if KDEV and KDEV < 3:
        return np.zeros((2, SEQ, D), np.float32)
    out = np.concatenate([res.results[c]["out"] for c in range(NCORE)], axis=0)
    return out.reshape(2, SEQ, D)


# version 02
# speedup vs baseline: 1.5800x; 1.5800x over previous
import os
import numpy as np
from contextlib import ExitStack

import concourse.bass as bass
import concourse.mybir as mybir
from concourse.bass_utils import run_bass_kernel_spmd

F32 = mybir.dt.float32
F32R = mybir.dt.float32r
BF16 = mybir.dt.bfloat16
U8 = mybir.dt.uint8
AF = mybir.ActivationFunctionType
ALU = mybir.AluOpType
AX = mybir.AxisListType

NCORE = 8
D = 1024
H3 = 3072
NH = 16
HD = 64
TC = 2048
NCH = 4
TCH = 512
NTT = 4
NT = NCH * NTT
IFF = 4096
NE = 4
NKT = 8
NIC = 32
SEQ = 8192

KDEV = int(os.environ.get("KDEV_PHASES", "0"))
KDEV_NOROPE = int(os.environ.get("KDEV_NOROPE", "0"))
KDEV_B2B = int(os.environ.get("KDEV_B2B", "1"))


def build():
    nc = bass.Bass("TRN2", target_bir_lowering=False,
                   detect_race_conditions=(os.environ.get("KDEV_NORD") != "1"))

    x_d = nc.dram_tensor("x", [TC, D], F32, kind="ExternalInput")
    wqkv_d = nc.dram_tensor("wqkv", [D, H3], F32, kind="ExternalInput")
    woa_d = nc.dram_tensor("woa", [D, D], F32, kind="ExternalInput")
    boa_d = nc.dram_tensor("boa", [1, D], F32, kind="ExternalInput")
    wr_d = nc.dram_tensor("wr", [D, NE], F32, kind="ExternalInput")
    w1_d = nc.dram_tensor("w1", [NE, D, IFF], F32, kind="ExternalInput")
    b1r_d = nc.dram_tensor("b1r", [128, 128], F32, kind="ExternalInput")
    wof_d = nc.dram_tensor("wof", [IFF, D], F32, kind="ExternalInput")
    bof_d = nc.dram_tensor("bof", [1, D], F32, kind="ExternalInput")
    gam_d = nc.dram_tensor("gam", [1, D], F32, kind="ExternalInput")
    bet_d = nc.dram_tensor("bet", [1, D], F32, kind="ExternalInput")
    cos_d = nc.dram_tensor("cosr", [TC, 512], F32, kind="ExternalInput")
    sin_d = nc.dram_tensor("sinr", [TC, 512], F32, kind="ExternalInput")
    idf_d = nc.dram_tensor("idf", [128, 128], F32, kind="ExternalInput")

    out_d = nc.dram_tensor("out", [TC, D], F32, kind="ExternalOutput")

    ccKTin = nc.dram_tensor("ccKTin", [D, TC], BF16, kind="Internal")
    ccVin = nc.dram_tensor("ccVin", [TC, D], BF16, kind="Internal")
    ccKTout = nc.dram_tensor("ccKTout", [4 * D, TC], BF16, kind="Internal")
    ccVout = nc.dram_tensor("ccVout", [4 * TC, D], BF16, kind="Internal")
    part_d = nc.dram_tensor("part", [TC, D], F32, kind="Internal")
    qT_d = nc.dram_tensor("qTd", [D, TC], BF16, kind="Internal")

    dbg = {}
    if KDEV:
        dbg["xnT"] = nc.dram_tensor("dbg_xnT", [128, NKT * TC], BF16, kind="ExternalOutput")
        dbg["qT"] = nc.dram_tensor("dbg_qT", [128, NKT * TC], BF16, kind="ExternalOutput")
        dbg["oh"] = nc.dram_tensor("dbg_oh", [1, NE * TC], BF16, kind="ExternalOutput")
        dbg["lg"] = nc.dram_tensor("dbg_lg", [128, NE * NT], F32, kind="ExternalOutput")
        dbg["int"] = nc.dram_tensor("dbg_int", [128, NIC * TCH], BF16, kind="ExternalOutput")
        dbg["msk"] = nc.dram_tensor("dbg_msk", [128, NE * TCH], U8, kind="ExternalOutput")
        dbg["part"] = nc.dram_tensor("dbg_part", [TC, D], F32, kind="ExternalOutput")
        dbg["attnT"] = nc.dram_tensor("dbg_attnT", [128, NKT * TC], BF16, kind="ExternalOutput")

    sems = {}

    def S(name):
        return sems[name]

    with ExitStack() as ST:
        E = ST.enter_context
        for n in ["sLD", "sWG", "sBC", "sBCe", "sX", "sSUM", "sMEAN", "sSSQ",
                  "sSD", "sLNd", "sTm", "sTEv", "sRm", "sOHr", "sOHm", "sOHe",
                  "sQKV", "sVE", "sVD", "sQSS", "sQSD", "sRPd", "sQKTm",
                  "sQKe", "sKV", "sCC", "sW1L", "sMm", "sMe", "sHm", "sGe",
                  "sSe", "sWOL", "sYm", "sYm8", "sE1", "sE2", "sEp", "sPA",
                  "sXR", "sKL", "sQD", "sSm", "sEx", "sPV", "sDv1", "sDv2",
                  "sDv3", "sAT", "sWOA", "sYa", "sFin", "sOut", "sDBG",
                  "sXa", "sXb", "sVDa", "sVDb", "w1s0", "w1s1", "w1s2", "w1s3",
                  "sWOa", "sWOb", "sKLa", "sKLb", "sOutA", "sOutB"]:
            sems[n] = E(nc.semaphore(n))

        xnT = E(nc.sbuf_tensor("xnT", [128, NKT * TC], BF16))
        mrstd = E(nc.sbuf_tensor("mrstd", [128, 2 * NT], F32))
        gam_bc = E(nc.sbuf_tensor("gam_bc", [128, D], F32))
        bet_bc = E(nc.sbuf_tensor("bet_bc", [128, D], F32))
        bof_bc = E(nc.sbuf_tensor("bof_bc", [128, D], F32))
        boa_bc = E(nc.sbuf_tensor("boa_bc", [128, D], F32))
        ohrow = E(nc.sbuf_tensor("ohrow", [1, NE * TC], BF16))
        b1r_sb = E(nc.sbuf_tensor("b1r_sb", [128, 128], F32))
        wr_sb = E(nc.sbuf_tensor("wr_sb", [128, NKT * NE], F32))
        idf_sb = E(nc.sbuf_tensor("idf_sb", [128, 128], F32))
        idb_sb = E(nc.sbuf_tensor("idb_sb", [128, 128], BF16))
        ones_f = E(nc.sbuf_tensor("ones_f", [1, 128], F32))
        ones_b = E(nc.sbuf_tensor("ones_b", [1, 128], BF16))
        ones_r = E(nc.sbuf_tensor("ones_r", [1, 128], F32R))
        rowbuf = E(nc.sbuf_tensor("rowbuf", [1, D], F32))
        epsc = E(nc.sbuf_tensor("epsc", [128, 4], F32))

        with nc.Block() as blk:
            @blk.sync
            def _(sync):
                sync.dma_start(idf_sb[:], idf_d.ap()).then_inc(S("sLD"), 16)
                sync.dma_start(b1r_sb[:], b1r_d.ap()).then_inc(S("sLD"), 16)
                sync.dma_start(wr_sb[:].rearrange("p (k n) -> p k n", k=NKT),
                               wr_d.ap().rearrange("(k p) n -> p k n", p=128)).then_inc(S("sLD"), 16)
                sync.wait_ge(S("sLD"), 48)

            @blk.vector
            def _(v):
                v.wait_ge(S("sLD"), 48)
                v.memset(ones_f[:], 1.0)
                v.memset(ones_b[:], 1.0)
                with nc.allow_low_precision(reason="f32r ones helper"):
                    v.tensor_copy(ones_r[:], ones_f[:])
                v.memset(epsc[:, 0:1], 1e-5)
                v.memset(epsc[:, 1:2], 1e-6)
                v.memset(epsc[:, 2:3], 1e-12)
                v.tensor_copy(idb_sb[:], idf_sb[:])

        with ExitStack() as S0:
            ps_bc = S0.enter_context(nc.psum_tensor("ps_bc", [128, 512], F32))
            with nc.Block() as blk:
                @blk.sync
                def _(sync):
                    for i, dr in enumerate([gam_d, bet_d, bof_d, boa_d]):
                        if i >= 1:
                            sync.wait_ge(S("sBC"), 2 * i)
                        sync.dma_start(rowbuf[:], dr.ap()).then_inc(S("sLD"), 16)

                @blk.tensor
                def _(t):
                    for i in range(4):
                        t.wait_ge(S("sLD"), 48 + (i + 1) * 16)
                        for n in range(2):
                            m = 2 * i + n
                            if m >= 1:
                                t.wait_ge(S("sBCe"), m)
                            mm = t.matmul(ps_bc[:], ones_f[:, 0:128],
                                          rowbuf[0:1, n * 512:(n + 1) * 512],
                                          start=True, stop=True)
                            mm.then_inc(S("sBC"), 1)

                @blk.vector
                def _(v):
                    for i, dst in enumerate([gam_bc, bet_bc, bof_bc, boa_bc]):
                        for n in range(2):
                            m = 2 * i + n
                            v.wait_ge(S("sBC"), m + 1)
                            cp = v.tensor_copy(dst[:, n * 512:(n + 1) * 512], ps_bc[:])
                            cp.then_inc(S("sBCe"), 1)

        with ExitStack() as S1:
            E1 = S1.enter_context
            wqkv_sb = E1(nc.sbuf_tensor("wqkv_sb", [128, NKT * H3], BF16))
            xt = [E1(nc.sbuf_tensor(f"xt{i}", [128, D], F32)) for i in range(2)]
            cos_sb = [E1(nc.sbuf_tensor(f"cosb{i}", [128, 512], F32)) for i in range(2)]
            sin_sb = [E1(nc.sbuf_tensor(f"sinb{i}", [128, 512], F32)) for i in range(2)]
            stats = E1(nc.sbuf_tensor("stats", [128, 8], F32))
            xc_t = E1(nc.sbuf_tensor("xc_t", [128, D], F32))
            sq_t = E1(nc.sbuf_tensor("sq_t", [128, D], F32))
            xnf = E1(nc.sbuf_tensor("xnf", [128, D], F32))
            xnTw = E1(nc.sbuf_tensor("xnTw", [128, NKT * TCH], F32))
            logt = E1(nc.sbuf_tensor("logt", [128, NE], F32))
            ohbf = E1(nc.sbuf_tensor("ohbf", [128, NE], BF16))
            q_ro = E1(nc.sbuf_tensor("q_ro", [128, D], F32))
            k_ro = E1(nc.sbuf_tensor("k_ro", [128, D], F32))
            rt_a = E1(nc.sbuf_tensor("rt_a", [128, D], F32))
            ss_q = E1(nc.sbuf_tensor("ss_q", [128, 3 * NH], F32))
            ss_k = E1(nc.sbuf_tensor("ss_k", [128, 3 * NH], F32))
            q_bf = E1(nc.sbuf_tensor("q_bf", [128, D], BF16))
            k_bf = E1(nc.sbuf_tensor("k_bf", [128, D], BF16))
            v_bf = [E1(nc.sbuf_tensor(f"v_bf{i}", [128, D], BF16)) for i in range(2)]
            ktst = E1(nc.sbuf_tensor("ktst", [128, NKT * TCH], BF16))
            lgst = E1(nc.sbuf_tensor("lgst", [128, NE * NT], F32)) if KDEV == 1 else None
            qtst = E1(nc.sbuf_tensor("qtst", [128, NKT * TCH], BF16))

            qkv_ps = E1(nc.psum_tensor("qkv_ps", [128, H3], F32))
            ps_m = E1(nc.psum_tensor("ps_m", [128, 512], F32))
            ps_mb = E1(nc.psum_tensor("ps_mb", [128, 1024], BF16))

            with nc.Block() as blk:
                @blk.gpsimd
                def _(g_):
                    g_.dma_start(
                        wqkv_sb[:].rearrange("p (k n) -> p k n", k=NKT),
                        wqkv_d.ap().rearrange("(k p) n -> p k n", p=128),
                    ).then_inc(S("sWG"), 16)

                @blk.sync
                def _(sync):
                    for g in range(NT):
                        r0 = g * 128
                        if g >= 2:
                            sync.wait_ge(S("sLNd"), g - 1)
                            sync.wait_ge(S("sRPd"), g - 1)
                        slot = S("sXa") if g % 2 == 0 else S("sXb")
                        sync.dma_start(xt[g % 2][:], x_d.ap()[r0:r0 + 128, :]).then_inc(slot, 16)
                        sync.dma_start(cos_sb[g % 2][:], cos_d.ap()[r0:r0 + 128, :]).then_inc(slot, 16)
                        sync.dma_start(sin_sb[g % 2][:], sin_d.ap()[r0:r0 + 128, :]).then_inc(slot, 16)
                        if g >= 1:
                            gv = g - 1
                            sync.wait_ge(S("sVE"), gv + 1)
                            sync.dma_start(ccVin.ap()[gv * 128:gv * 128 + 128, :],
                                           v_bf[gv % 2][:]).then_inc(S("sVDa") if gv % 2 == 0 else S("sVDb"), 16)
                        if g % NTT == 0 and g >= NTT:
                            ch = g // NTT - 1
                            sync.wait_ge(S("sQKe"), (ch + 1) * NTT * 2 * NKT)
                            for j in range(NKT):
                                sync.dma_start(
                                    ccKTin.ap()[j * 128:(j + 1) * 128, ch * TCH:(ch + 1) * TCH],
                                    ktst[:, j * TCH:(j + 1) * TCH]).then_inc(S("sKV"), 16)
                                sync.dma_start(
                                    qT_d.ap()[j * 128:(j + 1) * 128, ch * TCH:(ch + 1) * TCH],
                                    qtst[:, j * TCH:(j + 1) * TCH]).then_inc(S("sKV"), 16)
                    gv = NT - 1
                    sync.wait_ge(S("sVE"), gv + 1)
                    sync.dma_start(ccVin.ap()[gv * 128:gv * 128 + 128, :],
                                   v_bf[gv % 2][:]).then_inc(S("sVDa") if gv % 2 == 0 else S("sVDb"), 16)
                    ch = NCH - 1
                    sync.wait_ge(S("sQKe"), (ch + 1) * NTT * 2 * NKT)
                    for j in range(NKT):
                        sync.dma_start(
                            ccKTin.ap()[j * 128:(j + 1) * 128, ch * TCH:(ch + 1) * TCH],
                            ktst[:, j * TCH:(j + 1) * TCH]).then_inc(S("sKV"), 16)
                        sync.dma_start(
                            qT_d.ap()[j * 128:(j + 1) * 128, ch * TCH:(ch + 1) * TCH],
                            qtst[:, j * TCH:(j + 1) * TCH]).then_inc(S("sKV"), 16)

                @blk.vector
                def _(v):
                    for g in range(NT):
                        ch, tt = g // NTT, g % NTT
                        xtile = xt[g % 2]
                        v.wait_ge(S("sXa") if g % 2 == 0 else S("sXb"), (g // 2 + 1) * 48)
                        v.reduce_sum(stats[:, 0:1], xtile[:], AX.X).then_inc(S("sSUM"), 1)
                        v.wait_ge(S("sMEAN"), g + 1)
                        v.tensor_scalar(xc_t[:], xtile[:], stats[:, 1:2], None, ALU.subtract)
                        v.tensor_mul(sq_t[:], xc_t[:], xc_t[:])
                        v.reduce_sum(stats[:, 2:3], sq_t[:], AX.X).then_inc(S("sSSQ"), 1)
                        v.wait_ge(S("sSD"), g + 1)
                        v.reciprocal(stats[:, 4:5], stats[:, 3:4])
                        v.engine_nop().then_inc(S("sE1"), 1)
                        v.wait_ge(S("sE1"), 3 * g + 1)
                        v.tensor_copy(mrstd[:, 2 * g:2 * g + 1], stats[:, 1:2])
                        v.tensor_copy(mrstd[:, 2 * g + 1:2 * g + 2], stats[:, 4:5])
                        if g >= 1:
                            v.wait_ge(S("sTm"), g * NKT)
                        ts = v.tensor_scalar(xnf[:], xtile[:], stats[:, 1:2], stats[:, 4:5],
                                             ALU.subtract, ALU.mult)
                        ts.then_inc(S("sLNd"), 1)
                        if g >= 4:
                            v.wait_ge(S("sRm"), g - 3)
                        for j in range(NKT):
                            m = g * NKT + j
                            v.wait_ge(S("sTm"), m + 1)
                            v.tensor_copy(xnT[:, j * TC + g * 128:j * TC + g * 128 + 128],
                                          ps_m[:, 0:128])
                            cp = v.tensor_copy(xnTw[:, j * TCH + tt * 128:j * TCH + tt * 128 + 128],
                                               ps_m[:, 0:128])
                            cp.then_inc(S("sTEv"), 1)
                        v.wait_ge(S("sRm"), g + 1)
                        if KDEV == 1:
                            v.tensor_copy(lgst[:, g * NE:(g + 1) * NE], ps_m[:, 0:NE])
                        v.reduce_max(stats[:, 5:6], ps_m[:, 0:NE], AX.X)
                        v.engine_nop().then_inc(S("sE1"), 1)
                        v.wait_ge(S("sE1"), 3 * g + 2)
                        ts = v.tensor_scalar(ohbf[:], ps_m[:, 0:NE], stats[:, 5:6], None, ALU.is_ge)
                        ts.then_inc(S("sOHr"), 1)
                        for e in range(NE):
                            m = g * NE + e
                            v.wait_ge(S("sOHm"), m + 1)
                            cp = v.tensor_copy(ohrow[0:1, e * TC + g * 128:e * TC + g * 128 + 128],
                                               ps_mb[0:1, 0:128])
                            cp.then_inc(S("sOHe"), 1)
                        v.wait_ge(S("sQKV"), g + 1)
                        if g >= 1:
                            v.wait_ge(S("sQKTm"), g * 2 * NKT)
                        cosv = cos_sb[g % 2][:].rearrange("p (h u) -> p h u", u=32)
                        sinv = sin_sb[g % 2][:].rearrange("p (h u) -> p h u", u=32)
                        if KDEV_NOROPE:
                            v.tensor_copy(q_ro[:], qkv_ps[:, 0:D])
                            v.tensor_copy(k_ro[:], qkv_ps[:, D:2 * D])
                        for (base, ro) in (() if KDEV_NOROPE else ((0, q_ro), (D, k_ro))):
                            pv = qkv_ps[:, base:base + D].rearrange("p (h z u) -> p h z u", z=2, u=32)
                            rv = ro[:].rearrange("p (h z u) -> p h z u", z=2, u=32)
                            tv = rt_a[:].rearrange("p (h z u) -> p h z u", z=2, u=32)
                            v.tensor_tensor(tv[:, :, 0], pv[:, :, 0], cosv, ALU.mult)
                            v.tensor_tensor(tv[:, :, 1], pv[:, :, 1], sinv, ALU.mult)
                            v.tensor_tensor(rv[:, :, 0], tv[:, :, 0], tv[:, :, 1], ALU.subtract)
                            v.tensor_tensor(tv[:, :, 0], pv[:, :, 0], sinv, ALU.mult)
                            v.tensor_tensor(tv[:, :, 1], pv[:, :, 1], cosv, ALU.mult)
                            v.tensor_tensor(rv[:, :, 1], tv[:, :, 0], tv[:, :, 1], ALU.add)
                        for (ro, ss) in ((q_ro, ss_q), (k_ro, ss_k)):
                            v.tensor_mul(sq_t[:], ro[:], ro[:])
                            r = v.reduce_sum(ss[:, 0:NH].rearrange("p (h o) -> p h o", o=1),
                                             sq_t[:].rearrange("p (h u) -> p h u", u=HD), AX.X)
                            r.then_inc(S("sQSS"), 1)
                        v.wait_ge(S("sQSD"), 2 * g + 2)
                        v.reciprocal(ss_q[:, 2 * NH:3 * NH], ss_q[:, NH:2 * NH])
                        v.reciprocal(ss_k[:, 2 * NH:3 * NH], ss_k[:, NH:2 * NH])
                        v.engine_nop().then_inc(S("sE1"), 1)
                        v.wait_ge(S("sE1"), 3 * g + 3)
                        for (ro, ss, obf) in ((q_ro, ss_q, q_bf), (k_ro, ss_k, k_bf)):
                            for h in range(NH):
                                v.tensor_scalar(obf[:, h * HD:(h + 1) * HD],
                                                ro[:, h * HD:(h + 1) * HD],
                                                ss[:, 2 * NH + h:2 * NH + h + 1], None, ALU.mult)
                        v.engine_nop().then_inc(S("sRPd"), 1)
                        if tt == 0 and ch >= 1:
                            v.wait_ge(S("sKV"), ch * 2 * NKT * 16)
                        for j in range(2 * NKT):
                            m = g * 2 * NKT + j
                            v.wait_ge(S("sQKTm"), m + 1)
                            if j < NKT:
                                cp = v.tensor_copy(qtst[:, j * TCH + tt * 128:j * TCH + tt * 128 + 128],
                                                   ps_mb[:, 0:128])
                            else:
                                jj = j - NKT
                                cp = v.tensor_copy(ktst[:, jj * TCH + tt * 128:jj * TCH + tt * 128 + 128],
                                                   ps_mb[:, 0:128])
                            cp.then_inc(S("sQKe"), 1)

                @blk.scalar
                def _(sc):
                    for g in range(NT):
                        sc.wait_ge(S("sSUM"), g + 1)
                        sc.mul(stats[:, 1:2], stats[:, 0:1], 1.0 / D).then_inc(S("sMEAN"), 1)
                        sc.wait_ge(S("sSSQ"), g + 1)
                        a = sc.activation(stats[:, 3:4], stats[:, 2:3], AF.Sqrt,
                                          bias=epsc[:, 0:1], scale=1.0 / D)
                        a.then_inc(S("sSD"), 1)
                        sc.wait_ge(S("sQSS"), 2 * g + 1)
                        a = sc.activation(ss_q[:, NH:2 * NH], ss_q[:, 0:NH], AF.Sqrt, bias=epsc[:, 1:2])
                        a.then_inc(S("sQSD"), 1)
                        sc.wait_ge(S("sQSS"), 2 * g + 2)
                        a = sc.activation(ss_k[:, NH:2 * NH], ss_k[:, 0:NH], AF.Sqrt, bias=epsc[:, 1:2])
                        a.then_inc(S("sQSD"), 1)
                        sc.wait_ge(S("sQKV"), g + 1)
                        if g >= 2:
                            sc.wait_ge(S("sVDa") if g % 2 == 0 else S("sVDb"), (g // 2) * 16)
                        a = sc.copy(v_bf[g % 2][:], qkv_ps[:, 2 * D:3 * D])
                        a.then_inc(S("sVE"), 1)

                @blk.tensor
                def _(t):
                    for g in range(NT):
                        ch, tt = g // NTT, g % NTT
                        t.wait_ge(S("sLNd"), g + 1)
                        if g >= 1:
                            t.wait_ge(S("sOHr"), g)
                        for j in range(NKT):
                            m = g * NKT + j
                            if m >= 1:
                                t.wait_ge(S("sTEv"), m)
                            tr = t.transpose(ps_m[:, 0:128], xnf[:, j * 128:(j + 1) * 128], idf_sb[:])
                            tr.then_inc(S("sTm"), 1)
                        t.wait_ge(S("sTEv"), (g + 1) * NKT)
                        for k in range(NKT):
                            mm = t.matmul(ps_m[:, 0:NE],
                                          xnTw[:, k * TCH + tt * 128:k * TCH + tt * 128 + 128],
                                          wr_sb[:].rearrange("p (k2 n) -> p k2 n", k2=NKT)[:, k, :],
                                          start=(k == 0), stop=(k == NKT - 1))
                        mm.then_inc(S("sRm"), 1)
                        t.wait_ge(S("sOHr"), g + 1)
                        for e in range(NE):
                            if e == 0:
                                if g >= 1:
                                    t.wait_ge(S("sQKe"), g * 2 * NKT)
                            else:
                                t.wait_ge(S("sOHe"), g * NE + e)
                            tr = t.transpose(ps_mb[0:1, 0:128], ohbf[:, e:e + 1], idb_sb[:])
                            tr.then_inc(S("sOHm"), 1)
                        if g == 0:
                            t.wait_ge(S("sWG"), 16)
                        t.wait_ge(S("sOHe"), (g + 1) * NE)
                        if g >= 1:
                            t.wait_ge(S("sRPd"), g)
                            t.wait_ge(S("sVE"), g)
                        for n in range(6):
                            for k in range(NKT):
                                mm = t.matmul(qkv_ps[:, n * 512:(n + 1) * 512],
                                              xnT[:, k * TC + g * 128:k * TC + g * 128 + 128],
                                              wqkv_sb[:, k * H3 + n * 512:k * H3 + (n + 1) * 512],
                                              start=(k == 0), stop=(k == NKT - 1))
                        mm.then_inc(S("sQKV"), 1)
                        t.wait_ge(S("sRPd"), g + 1)
                        for j in range(2 * NKT):
                            m = g * 2 * NKT + j
                            if j == 0:
                                t.wait_ge(S("sOHe"), (g + 1) * NE)
                            else:
                                t.wait_ge(S("sQKe"), m)
                            src = q_bf if j < NKT else k_bf
                            jj = j % NKT
                            tr = t.transpose(ps_mb[:, 0:128], src[:, jj * 128:(jj + 1) * 128], idb_sb[:])
                            tr.then_inc(S("sQKTm"), 1)

        with nc.Block() as blk:
            @blk.gpsimd
            def _(g_):
                g_.wait_ge(S("sVDa"), NT * 8)
                g_.wait_ge(S("sVDb"), NT * 8)
                g_.wait_ge(S("sKV"), NCH * 2 * NKT * 16)
                g_.collective_compute(
                    "AllGather", ALU.bypass,
                    replica_groups=[[0, 1, 2, 3], [4, 5, 6, 7]],
                    ins=[ccKTin.ap().opt()], outs=[ccKTout.ap().opt()],
                ).then_inc(S("sCC"), 1)
                g_.collective_compute(
                    "AllGather", ALU.bypass,
                    replica_groups=[[0, 1, 2, 3], [4, 5, 6, 7]],
                    ins=[ccVin.ap().opt()], outs=[ccVout.ap().opt()],
                ).then_inc(S("sCC"), 1)

        if KDEV == 1:
            with nc.Block() as blk:
                @blk.sync
                def _(sync):
                    sync.dma_start(dbg["xnT"].ap(), xnT[:]).then_inc(S("sDBG"), 16)
                    sync.dma_start(dbg["qT"].ap(), qT_d.ap()).then_inc(S("sDBG"), 16)
                    sync.dma_start(dbg["oh"].ap(), ohrow[:]).then_inc(S("sDBG"), 16)
                    sync.dma_start(dbg["lg"].ap(), lgst[:]).then_inc(S("sDBG"), 16)
                    sync.wait_ge(S("sDBG"), 64)

        if KDEV == 0 or KDEV >= 2:
            with ExitStack() as S2:
                E2 = S2.enter_context
                w1sb = [E2(nc.sbuf_tensor(f"w1sb{i}", [128, NE * NKT * 128], BF16)) for i in range(4)]
                wofsb = [E2(nc.sbuf_tensor(f"wofsb{i}", [128, 8 * D], BF16)) for i in range(2)]
                intT = E2(nc.sbuf_tensor("intT", [128, NIC * TCH], BF16))
                mskst = E2(nc.sbuf_tensor("mskst", [128, NE * TCH], U8)) if KDEV == 2 else None
                g_sb = [E2(nc.sbuf_tensor(f"g_sb{i}", [128, TCH], BF16)) for i in range(NE)]
                seltmp = [E2(nc.sbuf_tensor(f"seltmp{i}", [128, TCH], BF16)) for i in range(2)]
                mask8 = [E2(nc.sbuf_tensor(f"mask8{i}", [128, TCH], U8)) for i in range(NE)]
                xr = [E2(nc.sbuf_tensor(f"xr{i}", [128, D], F32)) for i in range(2)]
                epi1 = E2(nc.sbuf_tensor("epi1", [128, D], F32))
                epi2 = E2(nc.sbuf_tensor("epi2", [128, D], F32))
                est = E2(nc.sbuf_tensor("est", [128, 8], F32))
                ptile = E2(nc.sbuf_tensor("ptile", [128, D], F32))

                for ch in range(NCH):
                    c0 = ch * TCH
                    with ExitStack() as SA:
                        hps = [SA.enter_context(nc.psum_tensor(f"hps{ch}_{i}", [128, TCH], F32))
                               for i in range(4)]
                        mps = SA.enter_context(nc.psum_tensor(f"mps{ch}", [128, TCH], F32))
                        with nc.Block() as blk:
                            @blk.gpsimd
                            def _(g_):
                                for grp in range(NIC):
                                    gg = ch * NIC + grp
                                    if gg >= 4:
                                        g_.wait_ge(S("sHm"), (gg - 3) * 4)
                                    for e in range(NE):
                                        g_.dma_start(
                                            w1sb[gg % 4][:].rearrange("p (e2 k n) -> p e2 k n",
                                                                      e2=NE, k=NKT)[:, e],
                                            w1_d.ap()[e].rearrange("(k p) i -> p k i", p=128)
                                            [:, :, grp * 128:(grp + 1) * 128],
                                        ).then_inc(S(f"w1s{gg % 4}"), 16)

                            @blk.tensor
                            def _(t):
                                for e in range(NE):
                                    if ch * NE + e >= 1:
                                        t.wait_ge(S("sMe"), ch * NE + e)
                                    mm = t.matmul(mps[:], ones_b[:, 0:128],
                                                  ohrow[0:1, e * TC + c0:e * TC + c0 + TCH],
                                                  start=True, stop=True)
                                    mm.then_inc(S("sMm"), 1)
                                for i in range(NIC):
                                    for e in range(NE):
                                        ie = ch * NIC * NE + i * NE + e
                                        gg2 = ch * NIC + i
                                        t.wait_ge(S(f"w1s{gg2 % 4}"), (gg2 // 4 + 1) * 64)
                                        if ie >= 4:
                                            t.wait_ge(S("sGe"), ie - 3)
                                        bv = w1sb[(ch * NIC + i) % 4][:].rearrange(
                                            "p (e2 k n) -> p e2 k n", e2=NE, k=NKT)
                                        for k in range(NKT):
                                            mm = t.matmul(
                                                hps[(i * NE + e) % 4][:],
                                                bv[:, e, k, :],
                                                xnT[:, k * TC + c0:k * TC + c0 + TCH],
                                                start=(k == 0), stop=(k == NKT - 1))
                                        mm.then_inc(S("sHm"), 1)

                            @blk.vector
                            def _(v):
                                for e in range(NE):
                                    v.wait_ge(S("sMm"), ch * NE + e + 1)
                                    ts = v.tensor_single_scalar(mask8[e][:], mps[:], 0.5, ALU.is_gt)
                                    ts.then_inc(S("sMe"), 1)
                                if KDEV == 2 and ch == 0:
                                    for e in range(NE):
                                        v.tensor_copy(mskst[:, e * TCH:(e + 1) * TCH], mask8[e][:])
                                    v.engine_nop().then_inc(S("sDBG"), 1)
                                for i in range(NIC):
                                    v.wait_ge(S("sGe"), ch * NIC * NE + i * NE + NE)
                                    v.select(seltmp[0][:], mask8[2][:], g_sb[2][:], g_sb[3][:])
                                    v.select(seltmp[1][:], mask8[1][:], g_sb[1][:], seltmp[0][:])
                                    s3 = v.select(intT[:, i * TCH:(i + 1) * TCH],
                                                  mask8[0][:], g_sb[0][:], seltmp[1][:])
                                    s3.then_inc(S("sSe"), 1)

                            @blk.scalar
                            def _(sc):
                                for i in range(NIC):
                                    for e in range(NE):
                                        ie = ch * NIC * NE + i * NE + e
                                        sc.wait_ge(S("sHm"), ie + 1)
                                        if ch * NIC + i >= 1:
                                            sc.wait_ge(S("sSe"), ch * NIC + i)
                                        a = sc.activation(g_sb[e][:], hps[(i * NE + e) % 4][:],
                                                          AF.Gelu,
                                                          bias=b1r_sb[:, e * 32 + i:e * 32 + i + 1])
                                        a.then_inc(S("sGe"), 1)

                    if KDEV == 2 and ch == 0:
                        with nc.Block() as blk:
                            @blk.sync
                            def _(sync):
                                sync.wait_ge(S("sSe"), NIC)
                                sync.dma_start(dbg["int"].ap(), intT[:]).then_inc(S("sDBG"), 16)
                    if not KDEV_B2B:
                        if ch < NCH - 1:
                            continue
                        with nc.Block() as blk:
                            @blk.sync
                            def _(sync):
                                sync.wait_ge(S("sSe"), NCH * NIC)
                                sync.dma_start(dbg["int"].ap(), intT[:]).then_inc(S("sDBG"), 16)
                            @blk.gpsimd
                            def _(g_):
                                if os.environ.get("KDEV_CCWAIT", "1") == "1":
                                    g_.wait_ge(S("sCC"), 2)
                        continue
                    with ExitStack() as SB:
                        yps = [SB.enter_context(nc.psum_tensor(f"yps{ch}_{i}", [128, D], F32))
                               for i in range(NTT)]
                        with nc.Block() as blk:
                            @blk.gpsimd
                            def _(g_):
                                for grp in range(4):
                                    gg = ch * 4 + grp
                                    if gg >= 2:
                                        g_.wait_ge(S("sYm8"), gg - 1)
                                    g_.dma_start(
                                        wofsb[grp % 2][:].rearrange("p (i n) -> p i n", i=8),
                                        wof_d.ap().rearrange("(i p) n -> p i n", p=128)
                                        [:, grp * 8:(grp + 1) * 8, :],
                                    ).then_inc(S("sWOa") if gg % 2 == 0 else S("sWOb"), 16)

                            @blk.tensor
                            def _(t):
                                for i in range(NIC):
                                    gg3 = ch * 4 + i // 8
                                    t.wait_ge(S("sWOa") if gg3 % 2 == 0 else S("sWOb"), (gg3 // 2 + 1) * 16)
                                    for tt in range(NTT):
                                        for n in range(2):
                                            mm = t.matmul(
                                                yps[tt][:, n * 512:(n + 1) * 512],
                                                intT[:, i * TCH + tt * 128:i * TCH + tt * 128 + 128],
                                                wofsb[(i // 8) % 2][:, (i % 8) * D + n * 512:
                                                                    (i % 8) * D + (n + 1) * 512],
                                                start=(i == 0), stop=(i == NIC - 1))
                                    if i % 8 == 7:
                                        mm.then_inc(S("sYm8"), 1)

                            @blk.sync
                            def _(sync):
                                for tt in range(NTT):
                                    g = ch * NTT + tt
                                    if tt >= 2:
                                        sync.wait_ge(S("sEp"), g - 1)
                                    sync.dma_start(xr[tt % 2][:],
                                                   x_d.ap()[g * 128:g * 128 + 128, :]).then_inc(S("sXR"), 16)
                                    if tt >= 1:
                                        sync.wait_ge(S("sEp"), g)
                                        sync.dma_start(part_d.ap()[(g - 1) * 128:g * 128, :],
                                                       ptile[:]).then_inc(S("sPA"), 16)
                                g = ch * NTT + NTT - 1
                                sync.wait_ge(S("sEp"), g + 1)
                                sync.dma_start(part_d.ap()[g * 128:(g + 1) * 128, :],
                                               ptile[:]).then_inc(S("sPA"), 16)

                            @blk.vector
                            def _(v):
                                for tt in range(NTT):
                                    g = ch * NTT + tt
                                    v.wait_ge(S("sYm8"), 4 * (ch + 1))
                                    v.wait_ge(S("sXR"), (g + 1) * 16)
                                    v.wait_ge(S("sPA"), g * 16)
                                    v.tensor_scalar(epi1[:], xr[tt % 2][:], mrstd[:, 2 * g:2 * g + 1],
                                                    mrstd[:, 2 * g + 1:2 * g + 2],
                                                    ALU.subtract, ALU.mult)
                                    v.tensor_tensor(epi2[:], yps[tt][:], bof_bc[:], ALU.add)
                                    v.tensor_tensor(epi1[:], epi1[:], epi2[:], ALU.add)
                                    v.reduce_sum(est[:, 0:1], epi1[:], AX.X).then_inc(S("sE1"), 1)
                                    v.wait_ge(S("sE2"), 2 * g + 1)
                                    v.tensor_scalar(epi2[:], epi1[:], est[:, 1:2], None, ALU.subtract)
                                    v.tensor_mul(epi1[:], epi2[:], epi2[:])
                                    v.reduce_sum(est[:, 2:3], epi1[:], AX.X).then_inc(S("sE1"), 1)
                                    v.wait_ge(S("sE2"), 2 * g + 2)
                                    v.reciprocal(est[:, 4:5], est[:, 3:4])
                                    v.engine_nop().then_inc(S("sE1"), 1)
                                    v.wait_ge(S("sE1"), 3 * NT + 3 * g + 3)
                                    v.scalar_tensor_tensor(epi1[:], epi2[:], est[:, 4:5], gam_bc[:],
                                                           ALU.mult, ALU.mult)
                                    v.tensor_tensor(epi1[:], epi1[:], bet_bc[:], ALU.add)
                                    ad = v.tensor_tensor(ptile[:], epi1[:], xr[tt % 2][:], ALU.add)
                                    ad.then_inc(S("sEp"), 1)

                            @blk.scalar
                            def _(sc):
                                for tt in range(NTT):
                                    g = ch * NTT + tt
                                    sc.wait_ge(S("sE1"), 3 * NT + 3 * g + 1)
                                    sc.mul(est[:, 1:2], est[:, 0:1], 1.0 / D).then_inc(S("sE2"), 1)
                                    sc.wait_ge(S("sE1"), 3 * NT + 3 * g + 2)
                                    a = sc.activation(est[:, 3:4], est[:, 2:3], AF.Sqrt,
                                                      bias=epsc[:, 2:3], scale=1.0 / D)
                                    a.then_inc(S("sE2"), 1)

            if KDEV == 2 and KDEV_B2B:
                with nc.Block() as blk:
                    @blk.gpsimd
                    def _(g_):
                        g_.wait_ge(S("sPA"), NT * 16)
                        g_.dma_start(dbg["part"].ap(), part_d.ap()).then_inc(S("sDBG"), 16)
                        g_.wait_ge(S("sDBG"), 17)
                        g_.dma_start(dbg["msk"].ap(), mskst[:]).then_inc(S("sDBG"), 16)
                        g_.dma_start(dbg["oh"].ap(), ohrow[:]).then_inc(S("sDBG"), 16)
                        g_.wait_ge(S("sDBG"), 49)

        if KDEV == 0 or KDEV >= 3:
            with ExitStack() as S3:
                E3 = S3.enter_context
                attnT = E3(nc.sbuf_tensor("attnT", [128, NKT * TC], BF16))
                kt2 = [E3(nc.sbuf_tensor(f"kt2_{i}", [128, 32 * 128], BF16)) for i in range(2)]
                vt = [E3(nc.sbuf_tensor(f"vt_{i}", [128, 64 * 65], BF16)) for i in range(2)]
                qdup = [E3(nc.sbuf_tensor(f"qdup{i}", [128, TC], BF16)) for i in range(2)]
                pT = [E3(nc.sbuf_tensor(f"pT{i}", [128, 1024], BF16)) for i in range(2)]
                rd_r = E3(nc.sbuf_tensor("rd_r", [1, 512], F32R))
                rdsb = E3(nc.sbuf_tensor("rdsb", [64, 512], F32))
                woasb = E3(nc.sbuf_tensor("woasb", [128, NKT * D], BF16))
                prt = [E3(nc.sbuf_tensor(f"prt{i}", [128, D], F32)) for i in range(2)]
                fin = [E3(nc.sbuf_tensor(f"fin{i}", [128, D], F32)) for i in range(2)]

                SH = ExitStack()
                sps = [SH.enter_context(nc.psum_tensor(f"sps{i}", [128, 1024], F32)) for i in range(2)]
                ovps = SH.enter_context(nc.psum_tensor("ovps", [65, 512], F32))
                rdps = SH.enter_context(nc.psum_tensor("rdps", [64, 512], F32))

                def head_loads(sync, h):
                    j, half = h // 2, (h % 2) * 64
                    slot = S("sKLa") if h % 2 == 0 else S("sKLb")
                    sync.dma_start(qdup[h % 2][0:64, :],
                                   qT_d.ap()[h * 64:h * 64 + 64, :]).then_inc(slot, 16)
                    sync.dma_start(qdup[h % 2][64:128, :],
                                   qT_d.ap()[h * 64:h * 64 + 64, :]).then_inc(slot, 16)
                    for r in range(4):
                        for par in range(2):
                            src = ccKTout.ap()[r * D + j * 128 + half:r * D + j * 128 + half + 64, :] \
                                .rearrange("d (m t2) -> d m t2", t2=128)[:, par::2, :]
                            dst = kt2[h % 2][par * 64:par * 64 + 64, r * 1024:(r + 1) * 1024] \
                                .rearrange("d (m t2) -> d m t2", t2=128)
                            sync.dma_start(dst, src).then_inc(slot, 16)
                    for r in range(4):
                        src = ccVout.ap()[r * TC:(r + 1) * TC, h * HD:(h + 1) * HD] \
                            .rearrange("(m p) d -> p m d", p=128)
                        dst = vt[h % 2][:].rearrange("p (m c) -> p m c", c=65)[:, 16 * r:16 * (r + 1), 0:64]
                        sync.dma_start(dst, src).then_inc(slot, 16)

                with nc.Block() as blk:
                    @blk.vector
                    def _(v):
                        for i in range(2):
                            v.memset(vt[i][:].rearrange("p (m c) -> p m c", c=65)[:, :, 64:65], 1.0)

                    @blk.gpsimd
                    def _(g_):
                        g_.dma_start(
                            woasb[:].rearrange("p (k n) -> p k n", k=NKT),
                            woa_d.ap().rearrange("(k p) n -> p k n", p=128),
                        ).then_inc(S("sWOA"), 16)

                for h in range(NH):
                    with nc.Block() as blk:
                        @blk.sync
                        def _(sync):
                            if h == 0:
                                sync.wait_ge(S("sCC"), 2)
                                head_loads(sync, 0)
                                head_loads(sync, 1)
                            elif h < NH - 1:
                                if h >= 1:
                                    sync.wait_ge(S("sPV"), 2 * 128 * h)
                                head_loads(sync, h + 1)

                        @blk.tensor
                        def _(t):
                            t.wait_ge(S("sKLa") if h % 2 == 0 else S("sKLb"), (h // 2 + 1) * 14 * 16)
                            for tc in range(4):
                                if h * 4 + tc >= 1:
                                    t.wait_ge(S("sAT"), h * 4 + tc)
                                    t.wait_ge(S("sDv3"), h * 4 + tc)
                                vv = vt[h % 2][:].rearrange("p (m2 c) -> p m2 c", c=65)
                                for m in range(34):
                                    if m < 32:
                                        gp = h * 128 + tc * 32 + m
                                        pb = m % 2
                                        if gp >= 2:
                                            t.wait_ge(S("sEx"), gp - 1)
                                        t.matmul(sps[pb][:, 0:512],
                                                 kt2[h % 2][0:64, m * 128:(m + 1) * 128],
                                                 qdup[h % 2][0:64, tc * 512:(tc + 1) * 512],
                                                 start=True, stop=True, tile_position=(0, 0))
                                        mm2 = t.matmul(sps[pb][:, 512:1024],
                                                       kt2[h % 2][64:128, m * 128:(m + 1) * 128],
                                                       qdup[h % 2][64:128, tc * 512:(tc + 1) * 512],
                                                       start=True, stop=True, tile_position=(64, 0))
                                        mm2.then_inc(S("sSm"), 1)
                                    if m >= 2:
                                        mp = m - 2
                                        gpp = h * 128 + tc * 32 + mp
                                        pbp = mp % 2
                                        t.wait_ge(S("sEx"), gpp + 1)
                                        mm = t.matmul(ovps[:], vv[:, 2 * mp, :], pT[pbp][:, 0:512],
                                                      start=(mp == 0), stop=False)
                                        mm.then_inc(S("sPV"), 1)
                                        mm = t.matmul(ovps[:], vv[:, 2 * mp + 1, :], pT[pbp][:, 512:1024],
                                                      start=False, stop=(mp == 31))
                                        mm.then_inc(S("sPV"), 1)
                                t.wait_ge(S("sDv1"), h * 4 + tc + 1)
                                mm = t.matmul(rdps[:], ones_r[0:1, 0:64], rd_r[0:1, :],
                                              start=True, stop=True)
                                mm.then_inc(S("sDv2"), 1)

                        @blk.scalar
                        def _(sc):
                            for tc in range(4):
                                hc = h * 4 + tc
                                for m in range(32):
                                    gp = h * 128 + tc * 32 + m
                                    pb = m % 2
                                    sc.wait_ge(S("sSm"), gp + 1)
                                    if gp >= 2:
                                        sc.wait_ge(S("sPV"), 2 * (gp - 1))
                                    a = sc.activation(pT[pb][:], sps[pb][:], AF.Exp, scale=0.125)
                                    a.then_inc(S("sEx"), 1)
                                sc.wait_ge(S("sDv2"), hc + 1)
                                if hc >= 1:
                                    sc.wait_ge(S("sAT"), hc)
                                a = sc.copy(rdsb[:], rdps[:])
                                a.then_inc(S("sDv3"), 1)

                        @blk.vector
                        def _(v):
                            jj, half = h // 2, (h % 2) * 64
                            for tc in range(4):
                                hc = h * 4 + tc
                                v.wait_ge(S("sPV"), h * 256 + (tc + 1) * 64)
                                with nc.allow_low_precision(reason="f32r bcast row for denom (1.2e-4 rel, fine)"):
                                    v.reciprocal(rd_r[0:1, :], ovps[64:65, :])
                                v.engine_nop().then_inc(S("sDv1"), 1)
                                v.wait_ge(S("sDv3"), hc + 1)
                                mu = v.tensor_tensor(
                                    attnT[half:half + 64, jj * TC + tc * 512:jj * TC + (tc + 1) * 512],
                                    ovps[0:64, :], rdsb[:], ALU.mult)
                                mu.then_inc(S("sAT"), 1)

                SH.close()
                with ExitStack() as SZ:
                    yaps = [SZ.enter_context(nc.psum_tensor(f"yaps{i}", [128, D], F32))
                            for i in range(2)]
                    with nc.Block() as blk:
                        @blk.tensor
                        def _(t):
                            t.wait_ge(S("sWOA"), 16)
                            for g in range(NT):
                                if g >= 2:
                                    t.wait_ge(S("sFin"), g - 1)
                                for k in range(NKT):
                                    for n in range(2):
                                        mm = t.matmul(yaps[g % 2][:, n * 512:(n + 1) * 512],
                                                      attnT[:, k * TC + g * 128:k * TC + g * 128 + 128],
                                                      woasb[:, k * D + n * 512:k * D + (n + 1) * 512],
                                                      start=(k == 0), stop=(k == NKT - 1))
                                mm.then_inc(S("sYa"), 1)

                        @blk.sync
                        def _(sync):
                            for g in range(NT):
                                if g == 0:
                                    sync.wait_ge(S("sPA"), NT * 16)
                                if g >= 2:
                                    sync.wait_ge(S("sFin"), g - 1)
                                sync.dma_start(prt[g % 2][:],
                                               part_d.ap()[g * 128:g * 128 + 128, :]).then_inc(S("sXR"), 16)
                                if g >= 1:
                                    sync.wait_ge(S("sFin"), g)
                                    sync.dma_start(out_d.ap()[(g - 1) * 128:g * 128, :],
                                                   fin[(g - 1) % 2][:]).then_inc(
                                        S("sOutA") if (g - 1) % 2 == 0 else S("sOutB"), 16)
                            g = NT - 1
                            sync.wait_ge(S("sFin"), g + 1)
                            sync.dma_start(out_d.ap()[g * 128:(g + 1) * 128, :],
                                           fin[g % 2][:]).then_inc(S("sOutB"), 16)
                            if KDEV >= 3:
                                sync.dma_start(dbg["attnT"].ap(), attnT[:]).then_inc(S("sOutB"), 16)
                                sync.wait_ge(S("sOutB"), NT * 8 + 16)
                            else:
                                sync.wait_ge(S("sOutB"), NT * 8)
                            sync.wait_ge(S("sOutA"), NT * 8)

                        @blk.vector
                        def _(v):
                            for g in range(NT):
                                v.wait_ge(S("sYa"), g + 1)
                                v.wait_ge(S("sXR"), (NT + g + 1) * 16)
                                if g >= 2:
                                    v.wait_ge(S("sOutA") if g % 2 == 0 else S("sOutB"),
                                              (g // 2) * 16)
                                v.tensor_tensor(fin[g % 2][:], yaps[g % 2][:], boa_bc[:], ALU.add)
                                ad = v.tensor_tensor(fin[g % 2][:], fin[g % 2][:], prt[g % 2][:], ALU.add)
                                ad.then_inc(S("sFin"), 1)

    nc.finalize()
    return nc


_NC_CACHE = {}


def kernel(L0_attn_x, L0_attn_Wqkv, L0_attn_Wo, L0_attn_bo, L0_ffn_W_router,
           L0_ffn_W1, L0_ffn_b1, L0_ffn_Wo, L0_ffn_bo, L0_ffn_ln_gamma,
           L0_ffn_ln_beta):
    x = np.ascontiguousarray(np.asarray(L0_attn_x, np.float32).reshape(NCORE * TC, D))
    wqkv = np.ascontiguousarray(np.asarray(L0_attn_Wqkv, np.float32))
    woa = np.ascontiguousarray(np.asarray(L0_attn_Wo, np.float32))
    boa = np.asarray(L0_attn_bo, np.float32).reshape(1, D)
    wr = np.ascontiguousarray(np.asarray(L0_ffn_W_router, np.float32))
    w1 = np.ascontiguousarray(np.asarray(L0_ffn_W1, np.float32))
    b1 = np.asarray(L0_ffn_b1, np.float32)
    b1r = np.ascontiguousarray(b1.reshape(NE, 32, 128).transpose(2, 0, 1).reshape(128, 128))
    wof = np.ascontiguousarray(np.asarray(L0_ffn_Wo, np.float32))
    bof = np.asarray(L0_ffn_bo, np.float32).reshape(1, D)
    gam = np.asarray(L0_ffn_ln_gamma, np.float32).reshape(1, D)
    bet = np.asarray(L0_ffn_ln_beta, np.float32).reshape(1, D)
    idf = np.eye(128, dtype=np.float32)

    inv = 1.0 / (10000.0 ** (np.arange(32, dtype=np.float64) / 32.0))
    in_maps = []
    for c in range(NCORE):
        pos = np.arange(TC, dtype=np.float64) + (c % 4) * TC
        ang = pos[:, None] * inv[None, :]
        cosr = np.ascontiguousarray(np.tile(np.cos(ang), (1, NH)).astype(np.float32))
        sinr = np.ascontiguousarray(np.tile(np.sin(ang), (1, NH)).astype(np.float32))
        in_maps.append({
            "x": x[c * TC:(c + 1) * TC], "wqkv": wqkv, "woa": woa, "boa": boa,
            "wr": wr, "w1": w1, "b1r": b1r, "wof": wof, "bof": bof,
            "gam": gam, "bet": bet, "cosr": cosr, "sinr": sinr, "idf": idf,
        })

    if KDEV not in _NC_CACHE:
        _NC_CACHE[KDEV] = build()
    nc = _NC_CACHE[KDEV]
    res = run_bass_kernel_spmd(nc, in_maps, core_ids=list(range(NCORE)))
    kernel._res = res
    if KDEV and KDEV < 3:
        return np.zeros((2, SEQ, D), np.float32)
    out = np.concatenate([res.results[c]["out"] for c in range(NCORE)], axis=0)
    return out.reshape(2, SEQ, D)


# version 03
# speedup vs baseline: 1.6287x; 1.0308x over previous
import os
import numpy as np
from contextlib import ExitStack

import concourse.bass as bass
import concourse.mybir as mybir
from concourse.bass_utils import run_bass_kernel_spmd

F32 = mybir.dt.float32
F32R = mybir.dt.float32r
BF16 = mybir.dt.bfloat16
U8 = mybir.dt.uint8
AF = mybir.ActivationFunctionType
ALU = mybir.AluOpType
AX = mybir.AxisListType

NCORE = 8
D = 1024
H3 = 3072
NH = 16
HD = 64
TC = 2048
NCH = 4
TCH = 512
NTT = 4
NT = NCH * NTT
IFF = 4096
NE = 4
NKT = 8
NIC = 32
SEQ = 8192

KDEV = int(os.environ.get("KDEV_PHASES", "0"))
KDEV_NOROPE = int(os.environ.get("KDEV_NOROPE", "0"))
KDEV_B2B = int(os.environ.get("KDEV_B2B", "1"))


def build():
    nc = bass.Bass("TRN2", target_bir_lowering=False,
                   detect_race_conditions=(os.environ.get("KDEV_NORD") != "1"))

    x_d = nc.dram_tensor("x", [TC, D], F32, kind="ExternalInput")
    wqkv_d = nc.dram_tensor("wqkv", [D, H3], F32, kind="ExternalInput")
    woa_d = nc.dram_tensor("woa", [D, D], F32, kind="ExternalInput")
    boa_d = nc.dram_tensor("boa", [1, D], F32, kind="ExternalInput")
    wr_d = nc.dram_tensor("wr", [D, NE], F32, kind="ExternalInput")
    w1_d = nc.dram_tensor("w1", [NE, D, IFF], F32, kind="ExternalInput")
    b1r_d = nc.dram_tensor("b1r", [128, 128], F32, kind="ExternalInput")
    wof_d = nc.dram_tensor("wof", [IFF, D], F32, kind="ExternalInput")
    bof_d = nc.dram_tensor("bof", [1, D], F32, kind="ExternalInput")
    gam_d = nc.dram_tensor("gam", [1, D], F32, kind="ExternalInput")
    bet_d = nc.dram_tensor("bet", [1, D], F32, kind="ExternalInput")
    cos_d = nc.dram_tensor("cosr", [TC, 512], F32, kind="ExternalInput")
    sin_d = nc.dram_tensor("sinr", [TC, 512], F32, kind="ExternalInput")
    idf_d = nc.dram_tensor("idf", [128, 128], F32, kind="ExternalInput")

    out_d = nc.dram_tensor("out", [TC, D], F32, kind="ExternalOutput")

    ccKTin = nc.dram_tensor("ccKTin", [D, TC], BF16, kind="Internal")
    ccVin = nc.dram_tensor("ccVin", [TC, D], BF16, kind="Internal")
    ccKTout = nc.dram_tensor("ccKTout", [4 * D, TC], BF16, kind="Internal")
    ccVout = nc.dram_tensor("ccVout", [4 * TC, D], BF16, kind="Internal")
    part_d = nc.dram_tensor("part", [TC, D], F32, kind="Internal")
    qT_d = nc.dram_tensor("qTd", [D, TC], BF16, kind="Internal")

    dbg = {}
    if KDEV:
        dbg["xnT"] = nc.dram_tensor("dbg_xnT", [128, NKT * TC], BF16, kind="ExternalOutput")
        dbg["qT"] = nc.dram_tensor("dbg_qT", [128, NKT * TC], BF16, kind="ExternalOutput")
        dbg["oh"] = nc.dram_tensor("dbg_oh", [1, NE * TC], BF16, kind="ExternalOutput")
        dbg["lg"] = nc.dram_tensor("dbg_lg", [128, NE * NT], F32, kind="ExternalOutput")
        dbg["int"] = nc.dram_tensor("dbg_int", [128, NIC * TCH], BF16, kind="ExternalOutput")
        dbg["msk"] = nc.dram_tensor("dbg_msk", [128, NE * TCH], U8, kind="ExternalOutput")
        dbg["part"] = nc.dram_tensor("dbg_part", [TC, D], F32, kind="ExternalOutput")
        dbg["attnT"] = nc.dram_tensor("dbg_attnT", [128, NKT * TC], BF16, kind="ExternalOutput")

    sems = {}

    def S(name):
        return sems[name]

    with ExitStack() as ST:
        E = ST.enter_context
        for n in ["sLD", "sWG", "sBC", "sBCe", "sX", "sSUM", "sMEAN", "sSSQ",
                  "sSD", "sLNd", "sTm", "sTEv", "sRm", "sOHr", "sOHm", "sOHe",
                  "sQKV", "sVE", "sVD", "sQSS", "sQSD", "sRPd", "sQKTm",
                  "sQKe", "sKV", "sCC", "sW1L", "sMm", "sMe", "sHm", "sGe",
                  "sSe", "sWOL", "sYm", "sYm8", "sE1", "sE2", "sEp", "sPA",
                  "sXR", "sKL", "sQD", "sSm", "sEx", "sPV", "sDv1", "sDv2",
                  "sDv3", "sAT", "sWOA", "sYa", "sFin", "sOut", "sDBG",
                  "sXa", "sXb", "sVDa", "sVDb", "w1s0", "w1s1", "w1s2", "w1s3",
                  "sWOa", "sWOb", "sKLa", "sKLb", "sOutA", "sOutB"]:
            sems[n] = E(nc.semaphore(n))

        xnT = E(nc.sbuf_tensor("xnT", [128, NKT * TC], BF16))
        mrstd = E(nc.sbuf_tensor("mrstd", [128, 2 * NT], F32))
        gam_bc = E(nc.sbuf_tensor("gam_bc", [128, D], F32))
        bet_bc = E(nc.sbuf_tensor("bet_bc", [128, D], F32))
        bof_bc = E(nc.sbuf_tensor("bof_bc", [128, D], F32))
        boa_bc = E(nc.sbuf_tensor("boa_bc", [128, D], F32))
        ohrow = E(nc.sbuf_tensor("ohrow", [1, NE * TC], BF16))
        b1r_sb = E(nc.sbuf_tensor("b1r_sb", [128, 128], F32))
        wr_sb = E(nc.sbuf_tensor("wr_sb", [128, NKT * NE], F32))
        idf_sb = E(nc.sbuf_tensor("idf_sb", [128, 128], F32))
        idb_sb = E(nc.sbuf_tensor("idb_sb", [128, 128], BF16))
        ones_f = E(nc.sbuf_tensor("ones_f", [1, 128], F32))
        ones_b = E(nc.sbuf_tensor("ones_b", [1, 128], BF16))
        ones_r = E(nc.sbuf_tensor("ones_r", [1, 128], F32R))
        rowbuf = E(nc.sbuf_tensor("rowbuf", [1, D], F32))
        epsc = E(nc.sbuf_tensor("epsc", [128, 4], F32))

        with nc.Block() as blk:
            @blk.sync
            def _(sync):
                sync.dma_start(idf_sb[:], idf_d.ap()).then_inc(S("sLD"), 16)
                sync.dma_start(b1r_sb[:], b1r_d.ap()).then_inc(S("sLD"), 16)
                sync.dma_start(wr_sb[:].rearrange("p (k n) -> p k n", k=NKT),
                               wr_d.ap().rearrange("(k p) n -> p k n", p=128)).then_inc(S("sLD"), 16)
                sync.wait_ge(S("sLD"), 48)

            @blk.vector
            def _(v):
                v.wait_ge(S("sLD"), 48)
                v.memset(ones_f[:], 1.0)
                v.memset(ones_b[:], 1.0)
                with nc.allow_low_precision(reason="f32r ones helper"):
                    v.tensor_copy(ones_r[:], ones_f[:])
                v.memset(epsc[:, 0:1], 1e-5)
                v.memset(epsc[:, 1:2], 1e-6)
                v.memset(epsc[:, 2:3], 1e-12)
                v.tensor_copy(idb_sb[:], idf_sb[:])

        with ExitStack() as S0:
            ps_bc = S0.enter_context(nc.psum_tensor("ps_bc", [128, 512], F32))
            with nc.Block() as blk:
                @blk.sync
                def _(sync):
                    for i, dr in enumerate([gam_d, bet_d, bof_d, boa_d]):
                        if i >= 1:
                            sync.wait_ge(S("sBC"), 2 * i)
                        sync.dma_start(rowbuf[:], dr.ap()).then_inc(S("sLD"), 16)

                @blk.tensor
                def _(t):
                    for i in range(4):
                        t.wait_ge(S("sLD"), 48 + (i + 1) * 16)
                        for n in range(2):
                            m = 2 * i + n
                            if m >= 1:
                                t.wait_ge(S("sBCe"), m)
                            mm = t.matmul(ps_bc[:], ones_f[:, 0:128],
                                          rowbuf[0:1, n * 512:(n + 1) * 512],
                                          start=True, stop=True)
                            mm.then_inc(S("sBC"), 1)

                @blk.vector
                def _(v):
                    for i, dst in enumerate([gam_bc, bet_bc, bof_bc, boa_bc]):
                        for n in range(2):
                            m = 2 * i + n
                            v.wait_ge(S("sBC"), m + 1)
                            cp = v.tensor_copy(dst[:, n * 512:(n + 1) * 512], ps_bc[:])
                            cp.then_inc(S("sBCe"), 1)

        with ExitStack() as S1:
            E1 = S1.enter_context
            wqkv_sb = E1(nc.sbuf_tensor("wqkv_sb", [128, NKT * H3], BF16))
            xt = [E1(nc.sbuf_tensor(f"xt{i}", [128, D], F32)) for i in range(2)]
            cos_sb = [E1(nc.sbuf_tensor(f"cosb{i}", [128, 512], F32)) for i in range(2)]
            sin_sb = [E1(nc.sbuf_tensor(f"sinb{i}", [128, 512], F32)) for i in range(2)]
            stats = E1(nc.sbuf_tensor("stats", [128, 8], F32))
            xc_t = E1(nc.sbuf_tensor("xc_t", [128, D], F32))
            sq_t = E1(nc.sbuf_tensor("sq_t", [128, D], F32))
            xnf = E1(nc.sbuf_tensor("xnf", [128, D], F32))
            xnTw = E1(nc.sbuf_tensor("xnTw", [128, NKT * TCH], F32))
            logt = E1(nc.sbuf_tensor("logt", [128, NE], F32))
            ohbf = E1(nc.sbuf_tensor("ohbf", [128, NE], BF16))
            q_ro = E1(nc.sbuf_tensor("q_ro", [128, D], F32))
            k_ro = E1(nc.sbuf_tensor("k_ro", [128, D], F32))
            rt_a = E1(nc.sbuf_tensor("rt_a", [128, D], F32))
            ss_q = E1(nc.sbuf_tensor("ss_q", [128, 3 * NH], F32))
            ss_k = E1(nc.sbuf_tensor("ss_k", [128, 3 * NH], F32))
            q_bf = E1(nc.sbuf_tensor("q_bf", [128, D], BF16))
            k_bf = E1(nc.sbuf_tensor("k_bf", [128, D], BF16))
            v_bf = [E1(nc.sbuf_tensor(f"v_bf{i}", [128, D], BF16)) for i in range(2)]
            ktst = E1(nc.sbuf_tensor("ktst", [128, NKT * TCH], BF16))
            lgst = E1(nc.sbuf_tensor("lgst", [128, NE * NT], F32)) if KDEV == 1 else None
            qtst = E1(nc.sbuf_tensor("qtst", [128, NKT * TCH], BF16))

            qkv_ps = E1(nc.psum_tensor("qkv_ps", [128, H3], F32))
            ps_m = E1(nc.psum_tensor("ps_m", [128, 512], F32))
            ps_mb = E1(nc.psum_tensor("ps_mb", [128, 1024], BF16))

            with nc.Block() as blk:
                @blk.gpsimd
                def _(g_):
                    g_.dma_start(
                        wqkv_sb[:].rearrange("p (k n) -> p k n", k=NKT),
                        wqkv_d.ap().rearrange("(k p) n -> p k n", p=128),
                    ).then_inc(S("sWG"), 16)

                @blk.sync
                def _(sync):
                    for g in range(NT):
                        r0 = g * 128
                        if g >= 2:
                            sync.wait_ge(S("sLNd"), g - 1)
                            sync.wait_ge(S("sRPd"), g - 1)
                        slot = S("sXa") if g % 2 == 0 else S("sXb")
                        sync.dma_start(xt[g % 2][:], x_d.ap()[r0:r0 + 128, :]).then_inc(slot, 16)
                        sync.dma_start(cos_sb[g % 2][:], cos_d.ap()[r0:r0 + 128, :]).then_inc(slot, 16)
                        sync.dma_start(sin_sb[g % 2][:], sin_d.ap()[r0:r0 + 128, :]).then_inc(slot, 16)
                        if g >= 1:
                            gv = g - 1
                            sync.wait_ge(S("sVE"), gv + 1)
                            sync.dma_start(ccVin.ap()[gv * 128:gv * 128 + 128, :],
                                           v_bf[gv % 2][:]).then_inc(S("sVDa") if gv % 2 == 0 else S("sVDb"), 16)
                        if g % NTT == 0 and g >= NTT:
                            ch = g // NTT - 1
                            sync.wait_ge(S("sQKe"), (ch + 1) * NTT * 2 * NKT)
                            for j in range(NKT):
                                sync.dma_start(
                                    ccKTin.ap()[j * 128:(j + 1) * 128, ch * TCH:(ch + 1) * TCH],
                                    ktst[:, j * TCH:(j + 1) * TCH]).then_inc(S("sKV"), 16)
                                sync.dma_start(
                                    qT_d.ap()[j * 128:(j + 1) * 128, ch * TCH:(ch + 1) * TCH],
                                    qtst[:, j * TCH:(j + 1) * TCH]).then_inc(S("sKV"), 16)
                    gv = NT - 1
                    sync.wait_ge(S("sVE"), gv + 1)
                    sync.dma_start(ccVin.ap()[gv * 128:gv * 128 + 128, :],
                                   v_bf[gv % 2][:]).then_inc(S("sVDa") if gv % 2 == 0 else S("sVDb"), 16)
                    ch = NCH - 1
                    sync.wait_ge(S("sQKe"), (ch + 1) * NTT * 2 * NKT)
                    for j in range(NKT):
                        sync.dma_start(
                            ccKTin.ap()[j * 128:(j + 1) * 128, ch * TCH:(ch + 1) * TCH],
                            ktst[:, j * TCH:(j + 1) * TCH]).then_inc(S("sKV"), 16)
                        sync.dma_start(
                            qT_d.ap()[j * 128:(j + 1) * 128, ch * TCH:(ch + 1) * TCH],
                            qtst[:, j * TCH:(j + 1) * TCH]).then_inc(S("sKV"), 16)

                @blk.vector
                def _(v):
                    for g in range(NT):
                        ch, tt = g // NTT, g % NTT
                        xtile = xt[g % 2]
                        v.wait_ge(S("sXa") if g % 2 == 0 else S("sXb"), (g // 2 + 1) * 48)
                        v.reduce_sum(stats[:, 0:1], xtile[:], AX.X).then_inc(S("sSUM"), 1)
                        v.wait_ge(S("sMEAN"), g + 1)
                        v.tensor_scalar(xc_t[:], xtile[:], stats[:, 1:2], None, ALU.subtract)
                        v.tensor_mul(sq_t[:], xc_t[:], xc_t[:])
                        v.reduce_sum(stats[:, 2:3], sq_t[:], AX.X).then_inc(S("sSSQ"), 1)
                        v.wait_ge(S("sSD"), g + 1)
                        v.reciprocal(stats[:, 4:5], stats[:, 3:4])
                        v.engine_nop().then_inc(S("sE1"), 1)
                        v.wait_ge(S("sE1"), 3 * g + 1)
                        v.tensor_copy(mrstd[:, 2 * g:2 * g + 1], stats[:, 1:2])
                        v.tensor_copy(mrstd[:, 2 * g + 1:2 * g + 2], stats[:, 4:5])
                        if g >= 1:
                            v.wait_ge(S("sTm"), g * NKT)
                        ts = v.tensor_scalar(xnf[:], xtile[:], stats[:, 1:2], stats[:, 4:5],
                                             ALU.subtract, ALU.mult)
                        ts.then_inc(S("sLNd"), 1)
                        if g >= 4:
                            v.wait_ge(S("sRm"), g - 3)
                        for j in range(NKT):
                            m = g * NKT + j
                            v.wait_ge(S("sTm"), m + 1)
                            v.tensor_copy(xnT[:, j * TC + g * 128:j * TC + g * 128 + 128],
                                          ps_m[:, 0:128])
                            cp = v.tensor_copy(xnTw[:, j * TCH + tt * 128:j * TCH + tt * 128 + 128],
                                               ps_m[:, 0:128])
                            cp.then_inc(S("sTEv"), 1)
                        v.wait_ge(S("sRm"), g + 1)
                        if KDEV == 1:
                            v.tensor_copy(lgst[:, g * NE:(g + 1) * NE], ps_m[:, 0:NE])
                        v.reduce_max(stats[:, 5:6], ps_m[:, 0:NE], AX.X)
                        v.engine_nop().then_inc(S("sE1"), 1)
                        v.wait_ge(S("sE1"), 3 * g + 2)
                        ts = v.tensor_scalar(ohbf[:], ps_m[:, 0:NE], stats[:, 5:6], None, ALU.is_ge)
                        ts.then_inc(S("sOHr"), 1)
                        for e in range(NE):
                            m = g * NE + e
                            v.wait_ge(S("sOHm"), m + 1)
                            cp = v.tensor_copy(ohrow[0:1, e * TC + g * 128:e * TC + g * 128 + 128],
                                               ps_mb[0:1, 0:128])
                            cp.then_inc(S("sOHe"), 1)
                        v.wait_ge(S("sQKV"), g + 1)
                        if g >= 1:
                            v.wait_ge(S("sQKTm"), g * 2 * NKT)
                        cosv = cos_sb[g % 2][:].rearrange("p (h u) -> p h u", u=32)
                        sinv = sin_sb[g % 2][:].rearrange("p (h u) -> p h u", u=32)
                        if KDEV_NOROPE:
                            v.tensor_copy(q_ro[:], qkv_ps[:, 0:D])
                            v.tensor_copy(k_ro[:], qkv_ps[:, D:2 * D])
                        for (base, ro) in (() if KDEV_NOROPE else ((0, q_ro), (D, k_ro))):
                            pv = qkv_ps[:, base:base + D].rearrange("p (h z u) -> p h z u", z=2, u=32)
                            rv = ro[:].rearrange("p (h z u) -> p h z u", z=2, u=32)
                            tv = rt_a[:].rearrange("p (h z u) -> p h z u", z=2, u=32)
                            v.tensor_tensor(tv[:, :, 0], pv[:, :, 0], cosv, ALU.mult)
                            v.tensor_tensor(tv[:, :, 1], pv[:, :, 1], sinv, ALU.mult)
                            v.tensor_tensor(rv[:, :, 0], tv[:, :, 0], tv[:, :, 1], ALU.subtract)
                            v.tensor_tensor(tv[:, :, 0], pv[:, :, 0], sinv, ALU.mult)
                            v.tensor_tensor(tv[:, :, 1], pv[:, :, 1], cosv, ALU.mult)
                            v.tensor_tensor(rv[:, :, 1], tv[:, :, 0], tv[:, :, 1], ALU.add)
                        for (ro, ss) in ((q_ro, ss_q), (k_ro, ss_k)):
                            v.tensor_mul(sq_t[:], ro[:], ro[:])
                            r = v.reduce_sum(ss[:, 0:NH].rearrange("p (h o) -> p h o", o=1),
                                             sq_t[:].rearrange("p (h u) -> p h u", u=HD), AX.X)
                            r.then_inc(S("sQSS"), 1)
                        v.wait_ge(S("sQSD"), 2 * g + 2)
                        v.reciprocal(ss_q[:, 2 * NH:3 * NH], ss_q[:, NH:2 * NH])
                        v.reciprocal(ss_k[:, 2 * NH:3 * NH], ss_k[:, NH:2 * NH])
                        v.engine_nop().then_inc(S("sE1"), 1)
                        v.wait_ge(S("sE1"), 3 * g + 3)
                        for (ro, ss, obf) in ((q_ro, ss_q, q_bf), (k_ro, ss_k, k_bf)):
                            for h in range(NH):
                                v.tensor_scalar(obf[:, h * HD:(h + 1) * HD],
                                                ro[:, h * HD:(h + 1) * HD],
                                                ss[:, 2 * NH + h:2 * NH + h + 1], None, ALU.mult)
                        v.engine_nop().then_inc(S("sRPd"), 1)
                        if tt == 0 and ch >= 1:
                            v.wait_ge(S("sKV"), ch * 2 * NKT * 16)
                        for j in range(2 * NKT):
                            m = g * 2 * NKT + j
                            v.wait_ge(S("sQKTm"), m + 1)
                            if j < NKT:
                                cp = v.tensor_copy(qtst[:, j * TCH + tt * 128:j * TCH + tt * 128 + 128],
                                                   ps_mb[:, 0:128])
                            else:
                                jj = j - NKT
                                cp = v.tensor_copy(ktst[:, jj * TCH + tt * 128:jj * TCH + tt * 128 + 128],
                                                   ps_mb[:, 0:128])
                            cp.then_inc(S("sQKe"), 1)

                @blk.scalar
                def _(sc):
                    for g in range(NT):
                        sc.wait_ge(S("sSUM"), g + 1)
                        sc.mul(stats[:, 1:2], stats[:, 0:1], 1.0 / D).then_inc(S("sMEAN"), 1)
                        sc.wait_ge(S("sSSQ"), g + 1)
                        a = sc.activation(stats[:, 3:4], stats[:, 2:3], AF.Sqrt,
                                          bias=epsc[:, 0:1], scale=1.0 / D)
                        a.then_inc(S("sSD"), 1)
                        sc.wait_ge(S("sQSS"), 2 * g + 1)
                        a = sc.activation(ss_q[:, NH:2 * NH], ss_q[:, 0:NH], AF.Sqrt, bias=epsc[:, 1:2])
                        a.then_inc(S("sQSD"), 1)
                        sc.wait_ge(S("sQSS"), 2 * g + 2)
                        a = sc.activation(ss_k[:, NH:2 * NH], ss_k[:, 0:NH], AF.Sqrt, bias=epsc[:, 1:2])
                        a.then_inc(S("sQSD"), 1)
                        sc.wait_ge(S("sQKV"), g + 1)
                        if g >= 2:
                            sc.wait_ge(S("sVDa") if g % 2 == 0 else S("sVDb"), (g // 2) * 16)
                        a = sc.copy(v_bf[g % 2][:], qkv_ps[:, 2 * D:3 * D])
                        a.then_inc(S("sVE"), 1)

                @blk.tensor
                def _(t):
                    for g in range(NT):
                        ch, tt = g // NTT, g % NTT
                        t.wait_ge(S("sLNd"), g + 1)
                        if g >= 1:
                            t.wait_ge(S("sOHr"), g)
                        for j in range(NKT):
                            m = g * NKT + j
                            if m >= 1:
                                t.wait_ge(S("sTEv"), m)
                            tr = t.transpose(ps_m[:, 0:128], xnf[:, j * 128:(j + 1) * 128], idf_sb[:])
                            tr.then_inc(S("sTm"), 1)
                        t.wait_ge(S("sTEv"), (g + 1) * NKT)
                        for k in range(NKT):
                            mm = t.matmul(ps_m[:, 0:NE],
                                          xnTw[:, k * TCH + tt * 128:k * TCH + tt * 128 + 128],
                                          wr_sb[:].rearrange("p (k2 n) -> p k2 n", k2=NKT)[:, k, :],
                                          start=(k == 0), stop=(k == NKT - 1))
                        mm.then_inc(S("sRm"), 1)
                        t.wait_ge(S("sOHr"), g + 1)
                        for e in range(NE):
                            if e == 0:
                                if g >= 1:
                                    t.wait_ge(S("sQKe"), g * 2 * NKT)
                            else:
                                t.wait_ge(S("sOHe"), g * NE + e)
                            tr = t.transpose(ps_mb[0:1, 0:128], ohbf[:, e:e + 1], idb_sb[:])
                            tr.then_inc(S("sOHm"), 1)
                        if g == 0:
                            t.wait_ge(S("sWG"), 16)
                        t.wait_ge(S("sOHe"), (g + 1) * NE)
                        if g >= 1:
                            t.wait_ge(S("sRPd"), g)
                            t.wait_ge(S("sVE"), g)
                        for n in range(6):
                            for k in range(NKT):
                                mm = t.matmul(qkv_ps[:, n * 512:(n + 1) * 512],
                                              xnT[:, k * TC + g * 128:k * TC + g * 128 + 128],
                                              wqkv_sb[:, k * H3 + n * 512:k * H3 + (n + 1) * 512],
                                              start=(k == 0), stop=(k == NKT - 1))
                        mm.then_inc(S("sQKV"), 1)
                        t.wait_ge(S("sRPd"), g + 1)
                        for j in range(2 * NKT):
                            m = g * 2 * NKT + j
                            if j == 0:
                                t.wait_ge(S("sOHe"), (g + 1) * NE)
                            else:
                                t.wait_ge(S("sQKe"), m)
                            src = q_bf if j < NKT else k_bf
                            jj = j % NKT
                            tr = t.transpose(ps_mb[:, 0:128], src[:, jj * 128:(jj + 1) * 128], idb_sb[:])
                            tr.then_inc(S("sQKTm"), 1)

        with nc.Block() as blk:
            @blk.gpsimd
            def _(g_):
                g_.wait_ge(S("sVDa"), NT * 8)
                g_.wait_ge(S("sVDb"), NT * 8)
                g_.wait_ge(S("sKV"), NCH * 2 * NKT * 16)
                g_.collective_compute(
                    "AllGather", ALU.bypass,
                    replica_groups=[[0, 1, 2, 3], [4, 5, 6, 7]],
                    ins=[ccKTin.ap().opt()], outs=[ccKTout.ap().opt()],
                ).then_inc(S("sCC"), 1)
                g_.collective_compute(
                    "AllGather", ALU.bypass,
                    replica_groups=[[0, 1, 2, 3], [4, 5, 6, 7]],
                    ins=[ccVin.ap().opt()], outs=[ccVout.ap().opt()],
                ).then_inc(S("sCC"), 1)

        if KDEV == 1:
            with nc.Block() as blk:
                @blk.sync
                def _(sync):
                    sync.dma_start(dbg["xnT"].ap(), xnT[:]).then_inc(S("sDBG"), 16)
                    sync.dma_start(dbg["qT"].ap(), qT_d.ap()).then_inc(S("sDBG"), 16)
                    sync.dma_start(dbg["oh"].ap(), ohrow[:]).then_inc(S("sDBG"), 16)
                    sync.dma_start(dbg["lg"].ap(), lgst[:]).then_inc(S("sDBG"), 16)
                    sync.wait_ge(S("sDBG"), 64)

        if KDEV == 0 or KDEV >= 2:
            with ExitStack() as S2:
                E2 = S2.enter_context
                w1sb = [E2(nc.sbuf_tensor(f"w1sb{i}", [128, NE * NKT * 128], BF16)) for i in range(4)]
                wofsb = [E2(nc.sbuf_tensor(f"wofsb{i}", [128, 8 * D], BF16)) for i in range(2)]
                intT = E2(nc.sbuf_tensor("intT", [128, NIC * TCH], BF16))
                mskst = E2(nc.sbuf_tensor("mskst", [128, NE * TCH], U8)) if KDEV == 2 else None
                g_sb = [E2(nc.sbuf_tensor(f"g_sb{i}", [128, TCH], BF16)) for i in range(NE)]
                seltmp = [E2(nc.sbuf_tensor(f"seltmp{i}", [128, TCH], BF16)) for i in range(2)]
                mask8 = [E2(nc.sbuf_tensor(f"mask8{i}", [128, TCH], U8)) for i in range(NE)]
                xr = [E2(nc.sbuf_tensor(f"xr{i}", [128, D], F32)) for i in range(2)]
                epi1 = E2(nc.sbuf_tensor("epi1", [128, D], F32))
                epi2 = E2(nc.sbuf_tensor("epi2", [128, D], F32))
                est = E2(nc.sbuf_tensor("est", [128, 8], F32))
                ptile = E2(nc.sbuf_tensor("ptile", [128, D], F32))

                for ch in range(NCH):
                    c0 = ch * TCH
                    with ExitStack() as SA:
                        hps = [SA.enter_context(nc.psum_tensor(f"hps{ch}_{i}", [128, TCH], F32))
                               for i in range(4)]
                        mps = SA.enter_context(nc.psum_tensor(f"mps{ch}", [128, TCH], F32))
                        with nc.Block() as blk:
                            @blk.gpsimd
                            def _(g_):
                                for grp in range(NIC):
                                    gg = ch * NIC + grp
                                    if gg >= 4:
                                        g_.wait_ge(S("sHm"), (gg - 3) * 4)
                                    for e in range(NE):
                                        g_.dma_start(
                                            w1sb[gg % 4][:].rearrange("p (e2 k n) -> p e2 k n",
                                                                      e2=NE, k=NKT)[:, e],
                                            w1_d.ap()[e].rearrange("(k p) i -> p k i", p=128)
                                            [:, :, grp * 128:(grp + 1) * 128],
                                        ).then_inc(S(f"w1s{gg % 4}"), 16)

                            @blk.tensor
                            def _(t):
                                for e in range(NE):
                                    if ch * NE + e >= 1:
                                        t.wait_ge(S("sMe"), ch * NE + e)
                                    mm = t.matmul(mps[:], ones_b[:, 0:128],
                                                  ohrow[0:1, e * TC + c0:e * TC + c0 + TCH],
                                                  start=True, stop=True)
                                    mm.then_inc(S("sMm"), 1)
                                for i in range(NIC):
                                    for e in range(NE):
                                        ie = ch * NIC * NE + i * NE + e
                                        gg2 = ch * NIC + i
                                        t.wait_ge(S(f"w1s{gg2 % 4}"), (gg2 // 4 + 1) * 64)
                                        if ie >= 4:
                                            t.wait_ge(S("sGe"), ie - 3)
                                        bv = w1sb[(ch * NIC + i) % 4][:].rearrange(
                                            "p (e2 k n) -> p e2 k n", e2=NE, k=NKT)
                                        for k in range(NKT):
                                            mm = t.matmul(
                                                hps[(i * NE + e) % 4][:],
                                                bv[:, e, k, :],
                                                xnT[:, k * TC + c0:k * TC + c0 + TCH],
                                                start=(k == 0), stop=(k == NKT - 1))
                                        mm.then_inc(S("sHm"), 1)

                            @blk.vector
                            def _(v):
                                for e in range(NE):
                                    v.wait_ge(S("sMm"), ch * NE + e + 1)
                                    ts = v.tensor_single_scalar(mask8[e][:], mps[:], 0.5, ALU.is_gt)
                                    ts.then_inc(S("sMe"), 1)
                                if KDEV == 2 and ch == 0:
                                    for e in range(NE):
                                        v.tensor_copy(mskst[:, e * TCH:(e + 1) * TCH], mask8[e][:])
                                    v.engine_nop().then_inc(S("sDBG"), 1)
                                for i in range(NIC):
                                    v.wait_ge(S("sGe"), ch * NIC * NE + i * NE + NE)
                                    v.select(seltmp[0][:], mask8[2][:], g_sb[2][:], g_sb[3][:])
                                    v.select(seltmp[1][:], mask8[1][:], g_sb[1][:], seltmp[0][:])
                                    s3 = v.select(intT[:, i * TCH:(i + 1) * TCH],
                                                  mask8[0][:], g_sb[0][:], seltmp[1][:])
                                    s3.then_inc(S("sSe"), 1)

                            @blk.scalar
                            def _(sc):
                                for i in range(NIC):
                                    for e in range(NE):
                                        ie = ch * NIC * NE + i * NE + e
                                        sc.wait_ge(S("sHm"), ie + 1)
                                        if ch * NIC + i >= 1:
                                            sc.wait_ge(S("sSe"), ch * NIC + i)
                                        a = sc.activation(g_sb[e][:], hps[(i * NE + e) % 4][:],
                                                          AF.Gelu,
                                                          bias=b1r_sb[:, e * 32 + i:e * 32 + i + 1])
                                        a.then_inc(S("sGe"), 1)

                    if KDEV == 2 and ch == 0:
                        with nc.Block() as blk:
                            @blk.sync
                            def _(sync):
                                sync.wait_ge(S("sSe"), NIC)
                                sync.dma_start(dbg["int"].ap(), intT[:]).then_inc(S("sDBG"), 16)
                    if not KDEV_B2B:
                        if ch < NCH - 1:
                            continue
                        with nc.Block() as blk:
                            @blk.sync
                            def _(sync):
                                sync.wait_ge(S("sSe"), NCH * NIC)
                                sync.dma_start(dbg["int"].ap(), intT[:]).then_inc(S("sDBG"), 16)
                            @blk.gpsimd
                            def _(g_):
                                if os.environ.get("KDEV_CCWAIT", "1") == "1":
                                    g_.wait_ge(S("sCC"), 2)
                        continue
                    with ExitStack() as SB:
                        yps = [SB.enter_context(nc.psum_tensor(f"yps{ch}_{i}", [128, D], F32))
                               for i in range(NTT)]
                        with nc.Block() as blk:
                            @blk.gpsimd
                            def _(g_):
                                for grp in range(4):
                                    gg = ch * 4 + grp
                                    if gg >= 2:
                                        g_.wait_ge(S("sYm8"), gg - 1)
                                    g_.dma_start(
                                        wofsb[grp % 2][:].rearrange("p (i n) -> p i n", i=8),
                                        wof_d.ap().rearrange("(i p) n -> p i n", p=128)
                                        [:, grp * 8:(grp + 1) * 8, :],
                                    ).then_inc(S("sWOa") if gg % 2 == 0 else S("sWOb"), 16)

                            @blk.tensor
                            def _(t):
                                for i in range(NIC):
                                    gg3 = ch * 4 + i // 8
                                    t.wait_ge(S("sWOa") if gg3 % 2 == 0 else S("sWOb"), (gg3 // 2 + 1) * 16)
                                    for tt in range(NTT):
                                        for n in range(2):
                                            mm = t.matmul(
                                                yps[tt][:, n * 512:(n + 1) * 512],
                                                intT[:, i * TCH + tt * 128:i * TCH + tt * 128 + 128],
                                                wofsb[(i // 8) % 2][:, (i % 8) * D + n * 512:
                                                                    (i % 8) * D + (n + 1) * 512],
                                                start=(i == 0), stop=(i == NIC - 1))
                                    if i % 8 == 7:
                                        mm.then_inc(S("sYm8"), 1)

                            @blk.sync
                            def _(sync):
                                for tt in range(NTT):
                                    g = ch * NTT + tt
                                    if tt >= 2:
                                        sync.wait_ge(S("sEp"), g - 1)
                                    sync.dma_start(xr[tt % 2][:],
                                                   x_d.ap()[g * 128:g * 128 + 128, :]).then_inc(S("sXR"), 16)
                                    if tt >= 1:
                                        sync.wait_ge(S("sEp"), g)
                                        sync.dma_start(part_d.ap()[(g - 1) * 128:g * 128, :],
                                                       ptile[:]).then_inc(S("sPA"), 16)
                                g = ch * NTT + NTT - 1
                                sync.wait_ge(S("sEp"), g + 1)
                                sync.dma_start(part_d.ap()[g * 128:(g + 1) * 128, :],
                                               ptile[:]).then_inc(S("sPA"), 16)

                            @blk.vector
                            def _(v):
                                for tt in range(NTT):
                                    g = ch * NTT + tt
                                    v.wait_ge(S("sYm8"), 4 * (ch + 1))
                                    v.wait_ge(S("sXR"), (g + 1) * 16)
                                    v.wait_ge(S("sPA"), g * 16)
                                    v.tensor_scalar(epi1[:], xr[tt % 2][:], mrstd[:, 2 * g:2 * g + 1],
                                                    mrstd[:, 2 * g + 1:2 * g + 2],
                                                    ALU.subtract, ALU.mult)
                                    v.tensor_tensor(epi2[:], yps[tt][:], bof_bc[:], ALU.add)
                                    v.tensor_tensor(epi1[:], epi1[:], epi2[:], ALU.add)
                                    v.reduce_sum(est[:, 0:1], epi1[:], AX.X).then_inc(S("sE1"), 1)
                                    v.wait_ge(S("sE2"), 2 * g + 1)
                                    v.tensor_scalar(epi2[:], epi1[:], est[:, 1:2], None, ALU.subtract)
                                    v.tensor_mul(epi1[:], epi2[:], epi2[:])
                                    v.reduce_sum(est[:, 2:3], epi1[:], AX.X).then_inc(S("sE1"), 1)
                                    v.wait_ge(S("sE2"), 2 * g + 2)
                                    v.reciprocal(est[:, 4:5], est[:, 3:4])
                                    v.engine_nop().then_inc(S("sE1"), 1)
                                    v.wait_ge(S("sE1"), 3 * NT + 3 * g + 3)
                                    v.scalar_tensor_tensor(epi1[:], epi2[:], est[:, 4:5], gam_bc[:],
                                                           ALU.mult, ALU.mult)
                                    v.tensor_tensor(epi1[:], epi1[:], bet_bc[:], ALU.add)
                                    ad = v.tensor_tensor(ptile[:], epi1[:], xr[tt % 2][:], ALU.add)
                                    ad.then_inc(S("sEp"), 1)

                            @blk.scalar
                            def _(sc):
                                for tt in range(NTT):
                                    g = ch * NTT + tt
                                    sc.wait_ge(S("sE1"), 3 * NT + 3 * g + 1)
                                    sc.mul(est[:, 1:2], est[:, 0:1], 1.0 / D).then_inc(S("sE2"), 1)
                                    sc.wait_ge(S("sE1"), 3 * NT + 3 * g + 2)
                                    a = sc.activation(est[:, 3:4], est[:, 2:3], AF.Sqrt,
                                                      bias=epsc[:, 2:3], scale=1.0 / D)
                                    a.then_inc(S("sE2"), 1)

            if KDEV == 2 and KDEV_B2B:
                with nc.Block() as blk:
                    @blk.gpsimd
                    def _(g_):
                        g_.wait_ge(S("sPA"), NT * 16)
                        g_.dma_start(dbg["part"].ap(), part_d.ap()).then_inc(S("sDBG"), 16)
                        g_.wait_ge(S("sDBG"), 17)
                        g_.dma_start(dbg["msk"].ap(), mskst[:]).then_inc(S("sDBG"), 16)
                        g_.dma_start(dbg["oh"].ap(), ohrow[:]).then_inc(S("sDBG"), 16)
                        g_.wait_ge(S("sDBG"), 49)

        if KDEV == 0 or KDEV >= 3:
            with ExitStack() as S3:
                E3 = S3.enter_context
                attnT = E3(nc.sbuf_tensor("attnT", [128, NKT * TC], BF16))
                kt2 = [E3(nc.sbuf_tensor(f"kt2_{i}", [128, 32 * 128], BF16)) for i in range(2)]
                vt = [E3(nc.sbuf_tensor(f"vt_{i}", [128, 64 * 65], BF16)) for i in range(2)]
                qdup = [E3(nc.sbuf_tensor(f"qdup{i}", [128, TC], BF16)) for i in range(2)]
                pT = [E3(nc.sbuf_tensor(f"pT{i}", [128, 1024], BF16)) for i in range(2)]
                rd_r = E3(nc.sbuf_tensor("rd_r", [1, 512], F32R))
                rdsb = E3(nc.sbuf_tensor("rdsb", [64, 512], F32))
                woasb = E3(nc.sbuf_tensor("woasb", [128, NKT * D], BF16))
                prt = [E3(nc.sbuf_tensor(f"prt{i}", [128, D], F32)) for i in range(2)]
                fin = [E3(nc.sbuf_tensor(f"fin{i}", [128, D], F32)) for i in range(2)]

                SH = ExitStack()
                sps = [SH.enter_context(nc.psum_tensor(f"sps{i}", [128, 1024], F32)) for i in range(2)]
                ovps = SH.enter_context(nc.psum_tensor("ovps", [65, 512], F32))
                rdps = SH.enter_context(nc.psum_tensor("rdps", [64, 512], F32))

                def head_loads(sync, h):
                    j, half = h // 2, (h % 2) * 64
                    slot = S("sKLa") if h % 2 == 0 else S("sKLb")
                    sync.dma_start(qdup[h % 2][0:64, :],
                                   qT_d.ap()[h * 64:h * 64 + 64, :]).then_inc(slot, 16)
                    sync.dma_start(qdup[h % 2][64:128, :],
                                   qT_d.ap()[h * 64:h * 64 + 64, :]).then_inc(slot, 16)
                    for r in range(4):
                        for par in range(2):
                            src = ccKTout.ap()[r * D + j * 128 + half:r * D + j * 128 + half + 64, :] \
                                .rearrange("d (m t2) -> d m t2", t2=128)[:, par::2, :]
                            dst = kt2[h % 2][par * 64:par * 64 + 64, r * 1024:(r + 1) * 1024] \
                                .rearrange("d (m t2) -> d m t2", t2=128)
                            sync.dma_start(dst, src).then_inc(slot, 16)
                    for r in range(4):
                        src = ccVout.ap()[r * TC:(r + 1) * TC, h * HD:(h + 1) * HD] \
                            .rearrange("(m p) d -> p m d", p=128)
                        dst = vt[h % 2][:].rearrange("p (m c) -> p m c", c=65)[:, 16 * r:16 * (r + 1), 0:64]
                        sync.dma_start(dst, src).then_inc(slot, 16)

                with nc.Block() as blk:
                    @blk.vector
                    def _(v):
                        for i in range(2):
                            v.memset(vt[i][:].rearrange("p (m c) -> p m c", c=65)[:, :, 64:65], 1.0)

                    @blk.gpsimd
                    def _(g_):
                        g_.dma_start(
                            woasb[:].rearrange("p (k n) -> p k n", k=NKT),
                            woa_d.ap().rearrange("(k p) n -> p k n", p=128),
                        ).then_inc(S("sWOA"), 16)

                with nc.Block() as blk:
                    @blk.sync
                    def _(sync):
                        sync.wait_ge(S("sCC"), 2)
                        head_loads(sync, 0)
                        head_loads(sync, 1)
                        for h in range(1, NH - 1):
                            sync.wait_ge(S("sPV"), 2 * 128 * h)
                            head_loads(sync, h + 1)

                    @blk.tensor
                    def _(t):
                        for h in range(NH):
                            t.wait_ge(S("sKLa") if h % 2 == 0 else S("sKLb"), (h // 2 + 1) * 14 * 16)
                            vv = vt[h % 2][:].rearrange("p (m2 c) -> p m2 c", c=65)
                            for tc in range(4):
                                for m in range(34):
                                    if m < 32:
                                        gp = h * 128 + tc * 32 + m
                                        pb = m % 2
                                        if gp >= 2:
                                            t.wait_ge(S("sEx"), gp - 1)
                                        t.matmul(sps[pb][:, 0:512],
                                                 kt2[h % 2][0:64, m * 128:(m + 1) * 128],
                                                 qdup[h % 2][0:64, tc * 512:(tc + 1) * 512],
                                                 start=True, stop=True, tile_position=(0, 0))
                                        mm2 = t.matmul(sps[pb][:, 512:1024],
                                                       kt2[h % 2][64:128, m * 128:(m + 1) * 128],
                                                       qdup[h % 2][64:128, tc * 512:(tc + 1) * 512],
                                                       start=True, stop=True, tile_position=(64, 0))
                                        mm2.then_inc(S("sSm"), 1)
                                    if m >= 2:
                                        mp = m - 2
                                        gpp = h * 128 + tc * 32 + mp
                                        pbp = mp % 2
                                        if mp == 0 and h * 4 + tc >= 1:
                                            t.wait_ge(S("sAT"), h * 4 + tc)
                                        t.wait_ge(S("sEx"), gpp + 1)
                                        mm = t.matmul(ovps[:], vv[:, 2 * mp, :], pT[pbp][:, 0:512],
                                                      start=(mp == 0), stop=False)
                                        mm.then_inc(S("sPV"), 1)
                                        mm = t.matmul(ovps[:], vv[:, 2 * mp + 1, :], pT[pbp][:, 512:1024],
                                                      start=False, stop=(mp == 31))
                                        mm.then_inc(S("sPV"), 1)
                                if h * 4 + tc >= 1:
                                    t.wait_ge(S("sDv3"), h * 4 + tc)
                                t.wait_ge(S("sDv1"), h * 4 + tc + 1)
                                mm = t.matmul(rdps[:], ones_r[0:1, 0:64], rd_r[0:1, :],
                                              start=True, stop=True)
                                mm.then_inc(S("sDv2"), 1)

                    @blk.scalar
                    def _(sc):
                        for h in range(NH):
                            for tc in range(4):
                                hc = h * 4 + tc
                                for m in range(32):
                                    gp = h * 128 + tc * 32 + m
                                    pb = m % 2
                                    sc.wait_ge(S("sSm"), gp + 1)
                                    if gp >= 2:
                                        sc.wait_ge(S("sPV"), 2 * (gp - 1))
                                    a = sc.activation(pT[pb][:], sps[pb][:], AF.Exp, scale=0.125)
                                    a.then_inc(S("sEx"), 1)
                                sc.wait_ge(S("sDv2"), hc + 1)
                                if hc >= 1:
                                    sc.wait_ge(S("sAT"), hc)
                                a = sc.copy(rdsb[:], rdps[:])
                                a.then_inc(S("sDv3"), 1)

                    @blk.vector
                    def _(v):
                        for h in range(NH):
                            jj, half = h // 2, (h % 2) * 64
                            for tc in range(4):
                                hc = h * 4 + tc
                                v.wait_ge(S("sPV"), h * 256 + (tc + 1) * 64)
                                with nc.allow_low_precision(reason="f32r bcast row for denom (1.2e-4 rel, fine)"):
                                    v.reciprocal(rd_r[0:1, :], ovps[64:65, :])
                                v.engine_nop().then_inc(S("sDv1"), 1)
                                v.wait_ge(S("sDv3"), hc + 1)
                                mu = v.tensor_tensor(
                                    attnT[half:half + 64, jj * TC + tc * 512:jj * TC + (tc + 1) * 512],
                                    ovps[0:64, :], rdsb[:], ALU.mult)
                                mu.then_inc(S("sAT"), 1)

                SH.close()
                with ExitStack() as SZ:
                    yaps = [SZ.enter_context(nc.psum_tensor(f"yaps{i}", [128, D], F32))
                            for i in range(2)]
                    with nc.Block() as blk:
                        @blk.tensor
                        def _(t):
                            t.wait_ge(S("sWOA"), 16)
                            for g in range(NT):
                                if g >= 2:
                                    t.wait_ge(S("sFin"), g - 1)
                                for k in range(NKT):
                                    for n in range(2):
                                        mm = t.matmul(yaps[g % 2][:, n * 512:(n + 1) * 512],
                                                      attnT[:, k * TC + g * 128:k * TC + g * 128 + 128],
                                                      woasb[:, k * D + n * 512:k * D + (n + 1) * 512],
                                                      start=(k == 0), stop=(k == NKT - 1))
                                mm.then_inc(S("sYa"), 1)

                        @blk.sync
                        def _(sync):
                            for g in range(NT):
                                if g == 0:
                                    sync.wait_ge(S("sPA"), NT * 16)
                                if g >= 2:
                                    sync.wait_ge(S("sFin"), g - 1)
                                sync.dma_start(prt[g % 2][:],
                                               part_d.ap()[g * 128:g * 128 + 128, :]).then_inc(S("sXR"), 16)
                                if g >= 1:
                                    sync.wait_ge(S("sFin"), g)
                                    sync.dma_start(out_d.ap()[(g - 1) * 128:g * 128, :],
                                                   fin[(g - 1) % 2][:]).then_inc(
                                        S("sOutA") if (g - 1) % 2 == 0 else S("sOutB"), 16)
                            g = NT - 1
                            sync.wait_ge(S("sFin"), g + 1)
                            sync.dma_start(out_d.ap()[g * 128:(g + 1) * 128, :],
                                           fin[g % 2][:]).then_inc(S("sOutB"), 16)
                            if KDEV >= 3:
                                sync.dma_start(dbg["attnT"].ap(), attnT[:]).then_inc(S("sOutB"), 16)
                                sync.wait_ge(S("sOutB"), NT * 8 + 16)
                            else:
                                sync.wait_ge(S("sOutB"), NT * 8)
                            sync.wait_ge(S("sOutA"), NT * 8)

                        @blk.vector
                        def _(v):
                            for g in range(NT):
                                v.wait_ge(S("sYa"), g + 1)
                                v.wait_ge(S("sXR"), (NT + g + 1) * 16)
                                if g >= 2:
                                    v.wait_ge(S("sOutA") if g % 2 == 0 else S("sOutB"),
                                              (g // 2) * 16)
                                v.tensor_tensor(fin[g % 2][:], yaps[g % 2][:], boa_bc[:], ALU.add)
                                ad = v.tensor_tensor(fin[g % 2][:], fin[g % 2][:], prt[g % 2][:], ALU.add)
                                ad.then_inc(S("sFin"), 1)

    nc.finalize()
    return nc


_NC_CACHE = {}


def kernel(L0_attn_x, L0_attn_Wqkv, L0_attn_Wo, L0_attn_bo, L0_ffn_W_router,
           L0_ffn_W1, L0_ffn_b1, L0_ffn_Wo, L0_ffn_bo, L0_ffn_ln_gamma,
           L0_ffn_ln_beta):
    x = np.ascontiguousarray(np.asarray(L0_attn_x, np.float32).reshape(NCORE * TC, D))
    wqkv = np.ascontiguousarray(np.asarray(L0_attn_Wqkv, np.float32))
    woa = np.ascontiguousarray(np.asarray(L0_attn_Wo, np.float32))
    boa = np.asarray(L0_attn_bo, np.float32).reshape(1, D)
    wr = np.ascontiguousarray(np.asarray(L0_ffn_W_router, np.float32))
    w1 = np.ascontiguousarray(np.asarray(L0_ffn_W1, np.float32))
    b1 = np.asarray(L0_ffn_b1, np.float32)
    b1r = np.ascontiguousarray(b1.reshape(NE, 32, 128).transpose(2, 0, 1).reshape(128, 128))
    wof = np.ascontiguousarray(np.asarray(L0_ffn_Wo, np.float32))
    bof = np.asarray(L0_ffn_bo, np.float32).reshape(1, D)
    gam = np.asarray(L0_ffn_ln_gamma, np.float32).reshape(1, D)
    bet = np.asarray(L0_ffn_ln_beta, np.float32).reshape(1, D)
    idf = np.eye(128, dtype=np.float32)

    inv = 1.0 / (10000.0 ** (np.arange(32, dtype=np.float64) / 32.0))
    in_maps = []
    for c in range(NCORE):
        pos = np.arange(TC, dtype=np.float64) + (c % 4) * TC
        ang = pos[:, None] * inv[None, :]
        cosr = np.ascontiguousarray(np.tile(np.cos(ang), (1, NH)).astype(np.float32))
        sinr = np.ascontiguousarray(np.tile(np.sin(ang), (1, NH)).astype(np.float32))
        in_maps.append({
            "x": x[c * TC:(c + 1) * TC], "wqkv": wqkv, "woa": woa, "boa": boa,
            "wr": wr, "w1": w1, "b1r": b1r, "wof": wof, "bof": bof,
            "gam": gam, "bet": bet, "cosr": cosr, "sinr": sinr, "idf": idf,
        })

    if KDEV not in _NC_CACHE:
        _NC_CACHE[KDEV] = build()
    nc = _NC_CACHE[KDEV]
    res = run_bass_kernel_spmd(nc, in_maps, core_ids=list(range(NCORE)))
    kernel._res = res
    if KDEV and KDEV < 3:
        return np.zeros((2, SEQ, D), np.float32)
    out = np.concatenate([res.results[c]["out"] for c in range(NCORE)], axis=0)
    return out.reshape(2, SEQ, D)
